# version 02
# baseline (speedup 1.0000x reference)
import numpy as np
import ml_dtypes
import concourse.bass as bass
import concourse.mybir as mybir
from concourse.bass_utils import run_bass_kernel_spmd

F32 = mybir.dt.float32
BF16 = mybir.dt.bfloat16
I32 = mybir.dt.int32
AF = mybir.ActivationFunctionType
ALU = mybir.AluOpType
AX = mybir.AxisListType
bfdt = ml_dtypes.bfloat16

NCORES = 8
TOK = 4096
SEQ = 2048
NH, HD, DS = 32, 64, 128
DI = 2048
SHIFT = 20.0
EPS = 1e-5

ENGINES = ("tensor", "vector", "scalar", "gpsimd", "sync")


class Sched:

    def __init__(self, nc):
        self.nc = nc
        self.q = {e: [] for e in ENGINES}
        self.sems = {}

    def sem(self, name):
        if name not in self.sems:
            self.sems[name] = [self.nc.alloc_semaphore(name=name), 0]
        return self.sems[name]

    def val(self, name):
        return self.sem(name)[1]

    def on(self, eng, fn):
        self.q[eng].append(fn)

    def wait(self, eng, name, v=None):
        s = self.sem(name)
        if v is None:
            v = s[1]
        if v is None or v <= 0:
            return
        self.on(eng, lambda e, h=s[0], vv=v: e.wait_ge(h, vv))

    def op(self, eng, fn, sem=None, n=1, drain=False):
        if sem is None:
            self.on(eng, fn)
            if drain:
                self.on(eng, lambda e: e.drain())
            return None
        s = self.sem(sem)
        s[1] += n
        if drain:
            self.on(eng, lambda e, f=fn: f(e))
            self.on(eng, lambda e, h=s[0], nn=n: e.drain().then_inc(h, nn))
        else:
            self.on(eng, lambda e, f=fn, h=s[0], nn=n: f(e).then_inc(h, nn))
        return s[1]

    def dma(self, eng, out, in_, sem=None, **kw):
        return self.op(eng, lambda e, o=out, i=in_: e.dma_start(o, i, **kw), sem, 16)

    def emit(self):
        with self.nc.Block() as block:
            for e in ENGINES:
                def run(eng_obj, items=tuple(self.q[e])):
                    for f in items:
                        f(eng_obj)
                getattr(block, e)(run)
        for e in ENGINES:
            self.q[e] = []


TENSORS = {
    "x_fm": ([128, 8, TOK], F32),
    "out_fm": ([128, 8, TOK], F32),
    "winT": ([1024, 4384], F32),
    "convw": ([18, 128, 4], F32),
    "convb": ([128, 18], F32),
    "dtb": ([32, 1], F32),
    "alog": ([32, 1], F32),
    "dcm": ([128, 16], F32),
    "nwcm": ([128, 16], F32),
    "woutT": ([DI, 1024], F32),
    "wgT": ([1024, 3072], I32),
    "wuT": ([1024, 3072], I32),
    "wdT": ([3072, 1024], I32),
    "wqT": ([1024, 1024], F32),
    "wkT": ([1024, 1024], F32),
    "wvT": ([1024, 1024], F32),
    "woT": ([1024, 1024], F32),
    "rwT": ([1024, 8], F32),
    "rb": ([1, 8], F32),
    "w1T": ([8, 1024, 8192], F32),
    "b1g": ([8, 128, 32], F32),
    "b1u": ([8, 128, 32], F32),
    "w2T": ([8, 4096, 1024], F32),
    "b2r": ([8, 128, 8], F32),
    "identf": ([128, 128], F32),
    "identb": ([128, 128], BF16),
    "trimask": ([128, 128], BF16),
    "cmask": ([4, 128, 512], BF16),
    "abias": ([128, 256], F32),
    "xn0": ([128, 8, TOK], BF16),
    "yg": ([128, 16, TOK], BF16),
    "x1": ([128, 8, TOK], F32),
    "xn1": ([128, 8, TOK], F32),
    "s1": ([128, 8, TOK], F32),
    "w1b": ([8, 1024, 8192], BF16),
    "w2b": ([8, 4096, 1024], BF16),
}

PHASE_IO = {
    "P": (["w1T", "w2T"], ["w1b", "w2b"]),
    "A": (["x_fm", "winT", "convw", "convb", "dtb", "alog", "dcm", "nwcm",
           "identf", "identb", "trimask"], ["xn0", "yg"]),
    "B": (["x_fm", "xn0", "yg", "woutT", "wgT", "wuT", "wdT"], ["x1", "xn1"]),
    "C": (["xn1", "wqT", "wkT", "wvT", "woT", "abias", "cmask"], ["s1"]),
    "D": (["xn1", "x1", "s1", "rwT", "rb", "w1b", "w2b", "b1g", "b1u", "b2r",
           "identf"], ["out_fm"]),
}


def host_prep(inputs):
    f32 = np.float32
    x = inputs["L0_attn_x"]
    shards = []
    for c in range(NCORES):
        xc = np.concatenate([x[2 * c].T, x[2 * c + 1].T], axis=1)
        shards.append(np.ascontiguousarray(xc.reshape(8, 128, TOK).transpose(1, 0, 2)))
    w = {}
    w["winT"] = np.ascontiguousarray(inputs["L0_attn_Win"].T)
    w["convw"] = np.ascontiguousarray(inputs["L0_attn_conv_w"].reshape(18, 128, 4))
    w["convb"] = np.ascontiguousarray(inputs["L0_attn_conv_b"].reshape(18, 128).T)
    w["dtb"] = np.ascontiguousarray(inputs["L0_attn_dt_bias"][:, None])
    w["alog"] = np.ascontiguousarray(inputs["L0_attn_A_log"][:, None])
    D = inputs["L0_attn_D"]
    w["dcm"] = np.ascontiguousarray(
        D[(2 * np.arange(16)[None, :] + (np.arange(128)[:, None] >= 64)).astype(int)]).astype(f32)
    w["nwcm"] = np.ascontiguousarray(inputs["L0_attn_norm_w"].reshape(16, 128).T)
    w["woutT"] = np.ascontiguousarray(inputs["L0_attn_Wout"].T)
    w["wgT"] = np.ascontiguousarray(inputs["L0_ffn_w_gate"].T)
    w["wuT"] = np.ascontiguousarray(inputs["L0_ffn_w_up"].T)
    w["wdT"] = np.ascontiguousarray(inputs["L0_ffn_w_down"].T)
    w["wqT"] = np.ascontiguousarray(inputs["L1_attn_Wq"].T)
    w["wkT"] = np.ascontiguousarray(inputs["L1_attn_Wk"].T)
    w["wvT"] = np.ascontiguousarray(inputs["L1_attn_Wv"].T)
    w["woT"] = np.ascontiguousarray(inputs["L1_attn_Wo"].T)
    w["rwT"] = np.ascontiguousarray(inputs["L1_ffn_router_w"].T)
    w["rb"] = np.ascontiguousarray(inputs["L1_ffn_router_b"][None, :])
    w["w1T"] = np.ascontiguousarray(inputs["L1_ffn_w1"].transpose(0, 2, 1))
    b1 = inputs["L1_ffn_b1"]
    w["b1g"] = np.ascontiguousarray(b1[:, :4096].reshape(8, 32, 128).transpose(0, 2, 1))
    w["b1u"] = np.ascontiguousarray(b1[:, 4096:].reshape(8, 32, 128).transpose(0, 2, 1))
    w["w2T"] = np.ascontiguousarray(inputs["L1_ffn_w2"].transpose(0, 2, 1))
    w["b2r"] = np.ascontiguousarray(inputs["L1_ffn_b2"].reshape(8, 8, 128).transpose(0, 2, 1))
    w["identf"] = np.eye(128, dtype=f32)
    w["identb"] = np.eye(128, dtype=f32).astype(bfdt)
    jj = np.arange(128)
    w["trimask"] = (jj[None, :] >= jj[:, None]).astype(f32).astype(bfdt)
    cm = np.zeros((4, 128, 512), np.float32)
    for d in range(4):
        cm[d] = (np.arange(512)[None, :] >= (jj[:, None] + d * 128)).astype(np.float32)
    w["cmask"] = cm.astype(bfdt)
    slopes = np.array([1.0 / ((2.0 ** 8) ** ((i + 1.0) / 16.0)) for i in range(16)], np.float64)
    ab = np.zeros((128, 256), np.float64)
    for h in range(16):
        for kt in range(16):
            ab[:, h * 16 + kt] = -slopes[h] * (kt * 128 + jj) - SHIFT
    w["abias"] = ab.astype(f32)
    return shards, w


def build_phaseP(S, io, alloc, psum):
    stg = [alloc("pstg%d" % i, [128, 8192], BF16) for i in range(2)]
    last_store = [0, 0]
    jobs = []
    for e in range(8):
        for kc in range(8):
            jobs.append((io["w1T"][e, kc * 128:(kc + 1) * 128, :],
                         io["w1b"][e, kc * 128:(kc + 1) * 128, :], False))
    for e in range(8):
        for half in range(4):
            src = io["w2T"][e].rearrange("(c p) d -> p c d", p=128)[:, half * 8:(half + 1) * 8, :]
            dst = io["w2b"][e].rearrange("(c p) d -> p c d", p=128)[:, half * 8:(half + 1) * 8, :]
            jobs.append((src, dst, True))
    for idx, (src, dst, three_d) in enumerate(jobs):
        i = idx % 2
        sview = stg[i].ap().rearrange("p (c d) -> p c d", d=1024) if three_d else stg[i][:, :]
        S.wait("gpsimd", "pst%d" % i, last_store[i])
        lv = S.dma("gpsimd", sview, src, "pld%d" % i)
        S.wait("sync", "pld%d" % i, lv)
        last_store[i] = S.dma("sync", dst, sview, "pst%d" % i)
    S.wait("sync", "pst0", S.val("pst0"))
    S.wait("sync", "pst1", S.val("pst1"))


def build_phaseB(S, io, alloc, psum):
    TB = 1024
    woutT = alloc("b_woutT", [128, 16, 1024], BF16)
    arena = alloc("b_arena", [128, 16 * 1024], BF16)
    h_sb = alloc("b_h", [128, 24 * 1024], BF16)
    sg_sb = alloc("b_sg", [128, 8 * 1024], BF16)
    x1_sb = alloc("b_x1", [128, 8 * 1024], F32)
    sil = [alloc("b_sil%d" % i, [128, 1024], BF16) for i in range(2)]
    wg_st = [alloc("b_wg%d" % i, [128, 8, 128], BF16) for i in range(2)]
    wu_st = [alloc("b_wu%d" % i, [128, 8, 128], BF16) for i in range(2)]
    wd_st = [alloc("b_wd%d" % i, [128, 24, 128], BF16) for i in range(2)]
    x_sb = [alloc("b_xin%d" % i, [128, 1024], F32) for i in range(2)]
    rows = alloc("b_rows", [1, 2 * 1024], F32)
    ones_f = alloc("b_ones", [128, 1], F32)
    rowt = alloc("b_rowt", [1, 1024], F32)
    ones_r = alloc("b_onesr", [1, 128], F32)
    epsc = alloc("b_eps", [1, 1], F32)
    tmp = h_sb.ap().bitcast(F32)

    S.op("vector", lambda e: e.memset(ones_f[:, :], 1.0))
    S.op("vector", lambda e: e.memset(ones_r[:, :], 1.0))
    S.op("vector", lambda e: e.memset(epsc[:, :], EPS), "dv")
    wout_ready = S.dma("gpsimd", woutT[:, :, :], io["woutT"].rearrange("(c p) d -> p c d", p=128), "wo")

    wgTv = io["wgT"].rearrange("(c p) h -> p c h", p=128)
    wuTv = io["wuT"].rearrange("(c p) h -> p c h", p=128)
    wdTv = io["wdT"].rearrange("(c p) d -> p c d", p=128)

    arena_free = 0
    psg_read = 0
    psd_read = 0
    arena_free = 0
    x_used = {}
    wbuf_used = [0, 0]
    dbuf_used = [0, 0]
    prev_s1x = 0

    for tb in range(4):
        t0 = tb * TB
        xn0v = arena.ap().rearrange("p (c t) -> p c t", t=1024)[:, 0:8, :]
        S.wait("sync", "sx", arena_free)
        ld_xn = S.dma("sync", xn0v, io["xn0"][:, :, t0:t0 + TB], "lxn")
        sil_done = {}
        h_done = {}
        for kc in range(24):
            ib = kc % 2
            gb = [psum[0], psum[1]] if ib == 0 else [psum[4], psum[5]]
            ub = [psum[2], psum[3]] if ib == 0 else [psum[6], psum[7]]
            S.wait("gpsimd", "pe", wbuf_used[ib])
            lg = S.dma("gpsimd", wg_st[ib][:, :, :], wgTv[:, :, kc * 128:(kc + 1) * 128], "w%d" % ib)
            lu = S.dma("gpsimd", wu_st[ib][:, :, :], wuTv[:, :, kc * 128:(kc + 1) * 128], "w%d" % ib)
            S.wait("tensor", "w%d" % ib, lu)
            if kc == 0:
                S.wait("tensor", "lxn", ld_xn)
            if kc >= 2:
                S.wait("tensor", "ac", sil_done[kc - 2])
                S.wait("tensor", "dv", h_done[kc - 2])
            elif tb > 0:
                S.wait("tensor", "dv", psg_read if ib == 0 else psd_read)
            for nb in range(2):
                for c in range(8):
                    S.op("tensor", lambda e, c=c, nb=nb, ib=ib, gb=gb: e.matmul(
                        gb[nb][:, :], wg_st[ib][:, c, :], xn0v[:, c, nb * 512:(nb + 1) * 512],
                        start=(c == 0), stop=(c == 7)))
            for nb in range(2):
                for c in range(8):
                    S.op("tensor", lambda e, c=c, nb=nb, ib=ib, ub=ub: e.matmul(
                        ub[nb][:, :], wu_st[ib][:, c, :], xn0v[:, c, nb * 512:(nb + 1) * 512],
                        start=(c == 0), stop=(c == 7)), "pe" if (c == 7 and nb == 1) else None)
            wbuf_used[ib] = S.val("pe")
            S.wait("scalar", "pe", S.val("pe"))
            for nb in range(2):
                sil_done[kc] = S.op("scalar", lambda e, nb=nb, ib=ib, gb=gb: e.activation(
                    sil[ib][:, nb * 512:(nb + 1) * 512], gb[nb][:, :], AF.Silu, scale=1.0 / 127.0), "ac")
            S.wait("vector", "ac", sil_done[kc])
            if kc == 0:
                S.wait("vector", "sx", arena_free if tb > 0 else 0)
            hv = h_sb[:, kc * 1024:(kc + 1) * 1024]
            for nb in range(2):
                h_done[kc] = S.op("vector", lambda e, nb=nb, ib=ib, ub=ub, hv=hv: e.scalar_tensor_tensor(
                    hv[:, nb * 512:(nb + 1) * 512], ub[nb][:, :], 1.0 / 127.0,
                    sil[ib][:, nb * 512:(nb + 1) * 512], ALU.mult, ALU.mult), "dv")
        S.wait("sync", "pe", S.val("pe"))
        ygv = arena.ap().rearrange("p (c t) -> p c t", t=1024)
        ld_yg = S.dma("sync", ygv, io["yg"][:, :, t0:t0 + TB], "lyg")
        hview = h_sb.ap().rearrange("p (c t) -> p c t", t=1024)
        dn_done = {}
        for dc in range(8):
            ib = dc % 2
            pd = [psum[0], psum[1]] if ib == 0 else [psum[2], psum[3]]
            S.wait("gpsimd", "pe", dbuf_used[ib])
            ldw = S.dma("gpsimd", wd_st[ib][:, :, :], wdTv[:, :, dc * 128:(dc + 1) * 128], "wd%d" % ib)
            S.wait("tensor", "wd%d" % ib, ldw)
            if dc == 0:
                S.wait("tensor", "dv", h_done[23])
                S.wait("tensor", "ac", sil_done[22])
            elif dc == 1:
                S.wait("tensor", "dv", h_done[22])
            else:
                S.wait("tensor", "ac", dn_done[dc - 2])
            for nb in range(2):
                for c in range(24):
                    S.op("tensor", lambda e, c=c, nb=nb, ib=ib, pd=pd: e.matmul(
                        pd[nb][:, :], wd_st[ib][:, c, :], hview[:, c, nb * 512:(nb + 1) * 512],
                        start=(c == 0), stop=(c == 23)), "pe" if (c == 23 and nb == 1) else None)
            dbuf_used[ib] = S.val("pe")
            S.wait("scalar", "pe", S.val("pe"))
            for nb in range(2):
                dn_done[dc] = S.op("scalar", lambda e, dc=dc, nb=nb, pd=pd: e.activation(
                    sg_sb[:, dc * 1024 + nb * 512:dc * 1024 + (nb + 1) * 512],
                    pd[nb][:, :], AF.Silu, scale=1.0 / 127.0), "ac")
        S.wait("tensor", "wo", wout_ready)
        S.wait("tensor", "lyg", ld_yg)
        x1done = {}
        ld_x = {}
        for dc in range(8):
            ib = dc % 2
            pw = [psum[4], psum[5]] if ib == 0 else [psum[6], psum[7]]
            S.wait("sync", "dv", x_used.get(dc - 2, 0))
            ld_x[dc] = S.dma("sync", x_sb[ib][:, :], io["x_fm"][:, dc, t0:t0 + TB], "lx%d" % ib)
            if dc == 0:
                S.wait("tensor", "ac", sil_done[23])
            elif dc == 1:
                S.wait("tensor", "dv", h_done[23])
            else:
                S.wait("tensor", "dv", x1done[dc - 2])
            for nb in range(2):
                for c in range(16):
                    S.op("tensor", lambda e, c=c, nb=nb, pw=pw, dc=dc: e.matmul(
                        pw[nb][:, :], woutT[:, c, dc * 128:(dc + 1) * 128],
                        ygv[:, c, nb * 512:(nb + 1) * 512],
                        start=(c == 0), stop=(c == 15)), "pe" if (c == 15 and nb == 1) else None)
            S.wait("vector", "pe", S.val("pe"))
            S.wait("vector", "ac", dn_done[dc])
            S.wait("vector", "lx%d" % ib, ld_x[dc])
            if dc == 0:
                S.wait("vector", "s1x", prev_s1x)
            for nb in range(2):
                sl = slice(dc * 1024 + nb * 512, dc * 1024 + (nb + 1) * 512)
                S.op("vector", lambda e, sl=sl, nb=nb, pw=pw: e.tensor_tensor(
                    x1_sb[:, sl], pw[nb][:, :], sg_sb[:, sl], ALU.mult), "dv")
            S.wait("vector", "dv", S.val("dv"))
            x1done[dc] = S.op("vector", lambda e, dc=dc, ib=ib: e.tensor_tensor(
                x1_sb[:, dc * 1024:(dc + 1) * 1024], x1_sb[:, dc * 1024:(dc + 1) * 1024],
                x_sb[ib][:, :], ALU.add), "dv")
            x_used[dc] = x1done[dc]
            S.wait("sync", "dv", x1done[dc])
            S.dma("sync", io["x1"][:, dc, t0:t0 + TB], x1_sb[:, dc * 1024:(dc + 1) * 1024], "s1x")
        prev_s1x = S.val("s1x")
        sq = tmp[:, 0:8192]
        S.wait("scalar", "dv", x1done[7])
        S.wait("scalar", "sx", arena_free)
        sq_done = S.op("scalar", lambda e: e.activation(sq[:, :], x1_sb[:, :], AF.Square), "ac")
        S.wait("tensor", "ac", sq_done)
        for nb in range(2):
            for c in range(8):
                S.op("tensor", lambda e, c=c, nb=nb: e.matmul(
                    psum[nb][0:1, :], ones_f[:, :],
                    x1_sb[:, c * 1024 + nb * 512:c * 1024 + (nb + 1) * 512],
                    start=(c == 0), stop=(c == 7)))
        for nb in range(2):
            for c in range(8):
                S.op("tensor", lambda e, c=c, nb=nb: e.matmul(
                    psum[2 + nb][0:1, :], ones_f[:, :],
                    sq[:, c * 1024 + nb * 512:c * 1024 + (nb + 1) * 512],
                    start=(c == 0), stop=(c == 7)), "pe" if (c == 7 and nb == 1) else None)
        S.wait("vector", "pe", S.val("pe"))
        mean = rows[:, 0:1024]; msq = rows[:, 1024:2048]
        rstd = msq; mrstd = mean
        for nb in range(2):
            S.op("vector", lambda e, nb=nb, mean=mean: e.tensor_scalar(
                mean[:, nb * 512:(nb + 1) * 512], psum[nb][0:1, :], 1.0 / 1024.0, None, ALU.mult), drain=(nb == 1))
            S.op("vector", lambda e, nb=nb, msq=msq: e.tensor_scalar(
                msq[:, nb * 512:(nb + 1) * 512], psum[2 + nb][0:1, :], 1.0 / 1024.0, None, ALU.mult), drain=(nb == 1))
        psg_read = S.op("vector", lambda e: e.tensor_tensor(rowt[:, :], mean[:, :], mean[:, :], ALU.mult), "dv", drain=True)
        S.op("vector", lambda e: e.tensor_sub(rstd[:, :], msq[:, :], rowt[:, :]), "dv", drain=True)

        S.wait("scalar", "dv", S.val("dv"))
        S.op("scalar", lambda e: e.activation(rstd[:, :], rstd[:, :], AF.Ln, bias=epsc[0:1, 0:1]), drain=True)
        rstd_done = S.op("scalar", lambda e: e.activation(rstd[:, :], rstd[:, :], AF.Exp, scale=-0.5), "ac", drain=True)
        S.wait("vector", "ac", rstd_done)
        mr_done = S.op("vector", lambda e: e.tensor_mul(mrstd[:, :], mean[:, :], rstd[:, :]), "dv")
        S.wait("tensor", "dv", mr_done)
        for nb in range(2):
            S.op("tensor", lambda e, nb=nb: e.matmul(
                psum[4 + nb][:, :], ones_r[:, :], rstd[0:1, nb * 512:(nb + 1) * 512],
                start=True, stop=True))
            S.op("tensor", lambda e, nb=nb: e.matmul(
                psum[6 + nb][:, :], ones_r[:, :], mrstd[0:1, nb * 512:(nb + 1) * 512],
                start=True, stop=True), "pe" if nb == 1 else None)
        S.wait("vector", "pe", S.val("pe"))
        for nb in range(2):
            x1v = x1_sb.ap().rearrange("p (c t) -> p c t", t=1024)[:, :, nb * 512:(nb + 1) * 512]
            tv = tmp.rearrange("p (c t) -> p c t", t=1024)[:, 0:8, nb * 512:(nb + 1) * 512]
            S.op("vector", lambda e, x1v=x1v, tv=tv, nb=nb: e.tensor_tensor(
                tv, x1v, psum[4 + nb][:, :].unsqueeze(1).broadcast_to([128, 8, 512]), ALU.mult), drain=True)
            psd_read = S.op("vector", lambda e, tv=tv, nb=nb: e.tensor_tensor(
                tv, tv, psum[6 + nb][:, :].unsqueeze(1).broadcast_to([128, 8, 512]), ALU.subtract), "dv")
        S.wait("sync", "dv", psd_read)
        arena_free = S.dma("sync", io["xn1"][:, :, t0:t0 + TB],
                           tmp[:, 0:8192].rearrange("p (c t) -> p c t", t=1024), "sx")
    S.wait("sync", "sx", S.val("sx"))
    S.wait("sync", "s1x", S.val("s1x"))



def build_phaseA(S, io, alloc, psum):
    T = 256
    winT = alloc("a_winT", [128, 8, 4384], BF16)
    x_sb = alloc("a_x", [128, 8, T], F32)
    xn_sb = alloc("a_xn", [128, 8, T], BF16)
    z_sb = alloc("a_z", [128, 16, T], BF16)
    xbc = alloc("a_xbc", [128, 18, T + 3], F32)
    xssm = alloc("a_xssm", [128, 16, T], BF16)
    b_sb = alloc("a_b", [128, T], BF16)
    c_sb = alloc("a_c", [128, T], BF16)
    dtp = alloc("a_dtp", [32, T], F32)
    dtt = alloc("a_dtt", [32, T], F32)
    la_sb = alloc("a_la", [32, T], F32)
    s_sb = alloc("a_s", [32, 128], F32)
    sT = alloc("a_sT", [128, 32], F32)
    dtpT = alloc("a_dtpT", [128, 32], F32)
    sball = alloc("a_sball", [128, 2048], F32)
    dt_m = alloc("a_dt", [128, 2048], F32)
    e_m = alloc("a_e", [128, 2048], BF16)
    ce_m = alloc("a_ce", [128, 2048], BF16)
    mt_m = alloc("a_mt", [128, 2048], BF16)
    decays = alloc("a_dec", [128, 32], F32)
    x_tm = alloc("a_xtm", [128, 2048], BF16)
    xs1 = alloc("a_xs1", [128, 2048], BF16)
    xs2 = alloc("a_xs2", [128, 1024], BF16)
    btm = alloc("a_btm", [128, 128], BF16)
    gtm = alloc("a_gtm", [128, 128], BF16)
    ht = alloc("a_ht", [128, 2048], F32)
    htb = alloc("a_htb", [128, 2048], BF16)
    y_sb = alloc("a_y", [128, 16, 128], F32)
    tmp = alloc("a_tmp", [128, 2048], F32)
    sz = alloc("a_sz", [128, 16, 128], BF16)
    yg_sb = alloc("a_yg", [128, 16, T], BF16)
    idf = alloc("a_idf", [128, 128], F32)
    idb = alloc("a_idb", [128, 128], BF16)
    tri = alloc("a_tri", [128, 128], BF16)
    convw = alloc("a_convw", [128, 18 * 4], F32)
    convb = alloc("a_convb", [128, 18], F32)
    dcm = alloc("a_dcm", [128, 16], F32)
    nwcm = alloc("a_nwcm", [128, 16], F32)
    dtb = alloc("a_dtb", [32, 1], F32)
    negA = alloc("a_negA", [32, 1], F32)
    zcol = alloc("a_zcol", [32, 1], F32)
    ones_c = alloc("a_onesc", [128, 1], F32)
    ones_r = alloc("a_onesr", [1, 128], F32)
    epsc = alloc("a_eps", [1, 1], F32)
    rows = alloc("a_rows", [1, 4 * T], F32)
    ctmp = [alloc("a_ctmp%d" % i, [128, T], F32) for i in range(2)]
    s_d = S.nc.dram_tensor("a_s_d", [32, 128], F32, kind="Internal").ap()

    mean = rows[:, 0:T]; msq = rows[:, T:2 * T]
    rowt = rows[:, 2 * T:3 * T]; rrow = rows[:, 3 * T:3 * T + 128]

    S.dma("gpsimd", winT[:, :, :], io["winT"].rearrange("(c p) d -> p c d", p=128), "wld")
    S.dma("sync", convw[:, :].rearrange("p (f k) -> p f k", k=4),
          io["convw"].rearrange("f p k -> p f k"), "li")
    S.dma("sync", convb[:, :], io["convb"], "li")
    S.dma("sync", dcm[:, :], io["dcm"], "li")
    S.dma("sync", nwcm[:, :], io["nwcm"], "li")
    S.dma("sync", dtb[:, :], io["dtb"], "li")
    S.dma("sync", negA[:, :], io["alog"], "li")
    S.dma("sync", idf[:, :], io["identf"], "li")
    S.dma("sync", idb[:, :], io["identb"], "li")
    ld_init = S.dma("sync", tri[:, :], io["trimask"], "li")
    S.op("vector", lambda e: e.memset(zcol[:, :], 0.0))
    S.op("vector", lambda e: e.memset(ones_c[:, :], 1.0))
    S.op("vector", lambda e: e.memset(ones_r[:, :], 1.0))
    S.op("vector", lambda e: e.memset(epsc[:, :], EPS), "dv")
    S.wait("scalar", "li", ld_init)
    S.op("scalar", lambda e: e.activation(negA[:, :], negA[:, :], AF.Exp), "ac", drain=True)
    S.wait("vector", "ac", S.val("ac"))
    S.op("vector", lambda e: e.tensor_scalar(negA[:, :], negA[:, :], -1.0, None, ALU.mult), "dv", drain=True)

    S.op("vector", lambda e: e.memset(ht[:, :], 0.0))
    S.op("vector", lambda e: e.memset(htb[:, :], 0.0), "dv")

    evac_done = {}
    prev = {}

    for ch in range(16):
        g0 = ch * T
        seq_start = (ch % 8 == 0)
        S.wait("sync", "dv", prev.get("x_free", 0))
        ld_x = S.dma("sync", x_sb[:, :, :], io["x_fm"][:, :, g0:g0 + T], "lx")
        S.wait("scalar", "lx", ld_x)
        S.wait("scalar", "dv", prev.get("tmp_free", S.val("dv")))
        sq_done = S.op("scalar", lambda e: e.activation(
            tmp[:, :], x_sb.ap().rearrange("p c t -> p (c t)"), AF.Square), "ac")
        S.wait("tensor", "ac", sq_done)
        S.wait("tensor", "dv", prev.get("b67_free", 0))
        for c in range(8):
            S.op("tensor", lambda e, c=c: e.matmul(
                psum[6][0:1, 0:T], ones_c[:, :], x_sb[:, c, :], start=(c == 0), stop=(c == 7)))
        for c in range(8):
            S.op("tensor", lambda e, c=c: e.matmul(
                psum[7][0:1, 0:T], ones_c[:, :], tmp[:, c * T:(c + 1) * T],
                start=(c == 0), stop=(c == 7)), "pe" if c == 7 else None)
        S.wait("vector", "pe", S.val("pe"))
        S.op("vector", lambda e: e.tensor_scalar(mean, psum[6][0:1, 0:T], 1.0 / 1024.0, None, ALU.mult), drain=True)
        S.op("vector", lambda e: e.tensor_scalar(msq, psum[7][0:1, 0:T], 1.0 / 1024.0, None, ALU.mult), drain=True)
        S.op("vector", lambda e: e.tensor_tensor(rowt, mean, mean, ALU.mult), drain=True)
        S.op("vector", lambda e: e.tensor_sub(rowt, msq, rowt), "dv", drain=True)
        S.wait("scalar", "dv", S.val("dv"))
        S.op("scalar", lambda e: e.activation(rowt, rowt, AF.Ln, bias=epsc[0:1, 0:1]), drain=True)
        S.op("scalar", lambda e: e.activation(rowt, rowt, AF.Exp, scale=-0.5), "ac", drain=True)
        S.wait("vector", "ac", S.val("ac"))
        S.op("vector", lambda e: e.tensor_mul(msq, mean, rowt), "dv", drain=True)
        S.wait("tensor", "dv", S.val("dv"))
        S.op("tensor", lambda e: e.matmul(psum[6][:, 0:T], ones_r[:, :], rowt, start=True, stop=True))
        S.op("tensor", lambda e: e.matmul(psum[7][:, 0:T], ones_r[:, :], msq, start=True, stop=True),
             "pe")
        S.wait("vector", "pe", S.val("pe"))
        S.wait("vector", "sxn", prev.get("xn_stored", 0))
        S.op("vector", lambda e: e.tensor_tensor(
            tmp[:, :].rearrange("p (c t) -> p c t", t=T), x_sb[:, :, :],
            psum[6][:, 0:T].unsqueeze(1).broadcast_to([128, 8, T]), ALU.mult), drain=True)
        xn_done = S.op("vector", lambda e: e.tensor_tensor(
            xn_sb[:, :, :], tmp[:, :].rearrange("p (c t) -> p c t", t=T),
            psum[7][:, 0:T].unsqueeze(1).broadcast_to([128, 8, T]), ALU.subtract), "dv")
        prev["b67_free"] = xn_done
        prev["x_free"] = xn_done
        S.wait("sync", "dv", xn_done)
        prev["xn_stored"] = S.dma("sync", io["xn0"][:, :, g0:g0 + T], xn_sb[:, :, :], "sxn")

        S.wait("vector", "dv", S.val("dv"))
        if seq_start:
            S.op("vector", lambda e: e.memset(xbc[:, :, 0:3], 0.0), "dv", drain=True)
        else:
            S.op("vector", lambda e: e.tensor_copy(xbc[:, :, 0:3], xbc[:, :, T:T + 3]), "dv", drain=True)
        halo_done = S.val("dv")

        S.wait("tensor", "wld", S.val("wld"))
        S.wait("tensor", "dv", xn_done)
        dt_raw_done = None
        for dc in range(35):
            bank = dc % 6
            S.wait("tensor", "dv", evac_done.get(("dv", bank), 0))
            S.wait("tensor", "ac", evac_done.get(("ac", bank), 0))
            M = 32 if dc == 34 else 128
            for c in range(8):
                S.op("tensor", lambda e, c=c, dc=dc, bank=bank, M=M: e.matmul(
                    psum[bank][0:M, 0:T], winT[:, c, dc * 128:dc * 128 + M], xn_sb[:, c, :],
                    start=(c == 0), stop=(c == 7)), "pe" if c == 7 else None)
            mmv = S.val("pe")
            if dc < 16:
                S.wait("vector", "pe", mmv)
                evac_done[("dv", bank)] = S.op("vector", lambda e, dc=dc, bank=bank: e.tensor_copy(
                    z_sb[:, dc, :], psum[bank][:, 0:T]), "dv")
            elif dc < 34:
                fc = dc - 16
                S.wait("vector", "pe", mmv)
                if fc == 0:
                    S.wait("vector", "dv", halo_done)
                evac_done[("dv", bank)] = S.op("vector", lambda e, fc=fc, bank=bank: e.tensor_copy(
                    xbc[:, fc, 3:T + 3], psum[bank][:, 0:T]), "dv")
            else:
                S.wait("vector", "pe", mmv)
                S.op("vector", lambda e, bank=bank: e.tensor_scalar(
                    dtp[:, :], psum[bank][0:32, 0:T], dtb[:, 0:1], None, ALU.add), drain=True)
                evac_done[("dv", bank)] = S.op("vector", lambda e: e.tensor_scalar(
                    dtt[:, :], dtp[:, :], 0.0, None, ALU.max), "dv", drain=True)
                dt_raw_done = evac_done[("dv", bank)]
        S.wait("scalar", "dv", dt_raw_done)
        S.op("scalar", lambda e: e.activation(la_sb[:, :], dtp[:, :], AF.Abs), drain=True)
        S.op("scalar", lambda e: e.activation(la_sb[:, :], la_sb[:, :], AF.Exp, scale=-1.0), drain=True)
        sp_done = S.op("scalar", lambda e: e.activation(la_sb[:, :], la_sb[:, :], AF.Ln, bias=1.0), "ac", drain=True)
        S.wait("vector", "ac", sp_done)
        S.op("vector", lambda e: e.tensor_add(dtp[:, :], dtt[:, :], la_sb[:, :]), drain=True)
        la_done = S.op("vector", lambda e: e.tensor_scalar(
            la_sb[:, :], dtp[:, :], negA[:, 0:1], None, ALU.mult), "dv", drain=True)

        conv_sil = {}
        S.wait("vector", "dv", S.val("dv"))
        for fc in range(18):
            cb = ctmp[fc % 2]
            wc = lambda k, fc=fc: convw[:, fc * 4 + k:fc * 4 + k + 1]
            S.wait("vector", "ac", conv_sil.get(fc - 2, 0))
            S.op("vector", lambda e, fc=fc, wc=wc, cb=cb: e.tensor_scalar(
                cb[:, :], xbc[:, fc, 0:T], wc(0), None, ALU.mult), drain=True)
            for k in (1, 2, 3):
                S.op("vector", lambda e, fc=fc, k=k, wc=wc, cb=cb: e.scalar_tensor_tensor(
                    cb[:, :], xbc[:, fc, k:T + k], wc(k), cb[:, :], ALU.mult, ALU.add),
                    "dv" if k == 3 else None, drain=True)
            S.wait("scalar", "dv", S.val("dv"))
            if fc < 16:
                dst = xssm[:, fc, :]
            elif fc == 16:
                dst = b_sb[:, :]
            else:
                dst = c_sb[:, :]
            conv_sil[fc] = S.op("scalar", lambda e, dst=dst, fc=fc, cb=cb: e.activation(
                dst, cb[:, :], AF.Silu, bias=convb[:, fc:fc + 1]), "ac")
        conv_done = S.val("ac")

        for q in range(2):
            sc_pe0 = S.val("pe")
            S.wait("vector", "ac", sp_done)
            scan_done = S.op("vector", lambda e, q=q: e.tensor_tensor_scan(
                s_sb[:, :], la_sb[:, q * 128:(q + 1) * 128],
                zcol[:, 0:1].broadcast_to([32, 128]), 0.0, ALU.add, ALU.add), "dv", drain=True)
            S.wait("sync", "dv", scan_done)
            s_st = S.dma("sync", s_d, s_sb[:, :], "ss")
            S.wait("tensor", "dv", max(scan_done, prev.get("xt_free", 0), prev.get("sdt_free", 0)))
            S.op("tensor", lambda e: e.transpose(psum[0].ap().bitcast(F32)[:, 0:32],
                                                 s_sb[:, :], idf[0:32, 0:32]))
            S.op("tensor", lambda e, q=q: e.transpose(psum[1].ap().bitcast(F32)[:, 0:32],
                                                      dtp[:, q * 128:(q + 1) * 128], idf[0:32, 0:32]), "pe")
            S.wait("vector", "pe", S.val("pe"))
            S.op("vector", lambda e: e.tensor_copy(sT[:, :], psum[0].ap().bitcast(F32)[:, 0:32]), drain=True)
            sdt_done = S.op("vector", lambda e: e.tensor_copy(dtpT[:, :], psum[1].ap().bitcast(F32)[:, 0:32]), "dv", drain=True)
            prev["sdt_free"] = sdt_done
            S.wait("tensor", "ac", conv_done)
            S.wait("tensor", "dv", sdt_done)
            xt_evac = {}
            for fc in range(16):
                bank = fc % 2
                S.wait("tensor", "dv", xt_evac.get(fc - 2, 0))
                S.op("tensor", lambda e, fc=fc, bank=bank, q=q: e.transpose(
                    psum[bank].ap().bitcast(BF16)[:, 0:128],
                    xssm[:, fc, q * 128:(q + 1) * 128], idb[:, :]), "pe")
                S.wait("vector", "pe", S.val("pe"))
                xt_evac[fc] = S.op("vector", lambda e, fc=fc, bank=bank: e.tensor_copy(
                    x_tm[:, fc * 128:(fc + 1) * 128], psum[bank].ap().bitcast(BF16)[:, 0:128]), "dv", drain=True)
            prev["xt_free"] = xt_evac[15]
            S.op("tensor", lambda e, q=q: e.transpose(
                psum[2].ap().bitcast(BF16)[:, 0:128], b_sb[:, q * 128:(q + 1) * 128], idb[:, :]))
            S.op("tensor", lambda e, q=q: e.matmul(
                psum[3][:, 0:128], b_sb[:, q * 128:(q + 1) * 128], c_sb[:, q * 128:(q + 1) * 128],
                start=True, stop=True), "pe")
            gt_mm = S.val("pe")
            S.wait("vector", "pe", gt_mm)
            btm_done = S.op("vector", lambda e: e.tensor_copy(btm[:, :], psum[2].ap().bitcast(BF16)[:, 0:128]), "dv", drain=True)
            gtm_done = S.op("vector", lambda e: e.tensor_mul(gtm[:, :], psum[3][:, 0:128], tri[:, :]), "dv", drain=True)
            S.wait("vector", "dv", xt_evac[15])
            xs1_done = S.op("vector", lambda e: e.tensor_tensor(
                xs1[:, :].rearrange("p (h w) -> p h w", w=64), x_tm[:, :].rearrange("p (h w) -> p h w", w=64),
                dtpT[:, :].unsqueeze(2).broadcast_to([128, 32, 64]), ALU.mult), "dv")

            for hh in range(2):
                S.wait("sync", "ss", s_st)
                S.wait("sync", "ac", prev.get("sball_free_ac", 0))
                S.wait("sync", "dv", prev.get("sball_free_dv", 0))
                srow = s_d.rearrange("h j -> (h j)")[hh * 2048:(hh + 1) * 2048]
                ld_sb = S.dma("sync", sball[:, :].unsqueeze(1),
                              srow.unsqueeze(0).partition_broadcast(128), "lsb")
                S.wait("vector", "lsb", ld_sb)
                S.wait("vector", "ac", prev.get("dt_free_ac", 0))
                S.op("vector", lambda e, hh=hh: e.tensor_tensor(
                    dt_m[:, :].rearrange("p (h i) -> p h i", i=128),
                    sball[:, :].rearrange("p (h i) -> p h i", i=128),
                    sT[:, hh * 16:(hh + 1) * 16].unsqueeze(2).broadcast_to([128, 16, 128]),
                    ALU.subtract), drain=True)
                dt_done = S.op("vector", lambda e: e.tensor_scalar(
                    dt_m[:, :], dt_m[:, :], 0.0, None, ALU.min), "dv")
                S.wait("scalar", "dv", dt_done)
                epos_done = S.op("scalar", lambda e: e.activation(sball[:, :], sball[:, :], AF.Exp), "ac")
                prev["sball_free_ac"] = epos_done
                e_done = S.op("scalar", lambda e: e.activation(e_m[:, :], dt_m[:, :], AF.Exp), "ac")
                prev["dt_free_ac"] = e_done
                S.wait("vector", "ac", e_done)
                ce_done = S.op("vector", lambda e, q=q: e.tensor_tensor(
                    ce_m[:, :].rearrange("p (h i) -> p h i", i=128),
                    c_sb[:, q * 128:(q + 1) * 128].unsqueeze(1).broadcast_to([128, 16, 128]),
                    sball[:, :].rearrange("p (h i) -> p h i", i=128), ALU.mult), "dv")
                dec_done = S.op("vector", lambda e, hh=hh: e.tensor_copy(
                    decays[:, hh * 16:(hh + 1) * 16].unsqueeze(2),
                    sball[:, :].rearrange("p (h i) -> p h i", i=128)[:, :, 127:128]), "dv", drain=True)
                prev["sball_free_dv"] = dec_done
                S.wait("vector", "dv", gtm_done)
                mt_done = S.op("vector", lambda e: e.tensor_tensor(
                    mt_m[:, :].rearrange("p (h i) -> p h i", i=128),
                    e_m[:, :].rearrange("p (h i) -> p h i", i=128),
                    gtm[:, :].unsqueeze(1).broadcast_to([128, 16, 128]), ALU.mult), "dv")
                xs2_done = S.op("vector", lambda e, hh=hh: e.tensor_tensor(
                    xs2[:, :].rearrange("p (h w) -> p h w", w=64),
                    xs1[:, hh * 1024:(hh + 1) * 1024].rearrange("p (h w) -> p h w", w=64),
                    e_m[:, :].rearrange("p (h i) -> p h i", i=128)[:, :, 127:128].broadcast_to([128, 16, 64]),
                    ALU.mult), "dv")
                S.wait("tensor", "dv", mt_done)
                S.wait("tensor", "dv", prev.get("y_free", 0))
                for hp in range(16):
                    h = hh * 16 + hp
                    bank = 4 + hp // 8
                    quad = (hp // 2) % 4
                    pslot = hp % 2
                    S.op("tensor", lambda e, h=h, hp=hp, bank=bank, quad=quad, pslot=pslot: e.matmul(
                        psum[bank][pslot * 64:(pslot + 1) * 64, quad * 128:(quad + 1) * 128],
                        xs1[:, h * 64:(h + 1) * 64], mt_m[:, hp * 128:(hp + 1) * 128],
                        start=True, stop=False))
                    S.op("tensor", lambda e, h=h, hp=hp, bank=bank, quad=quad, pslot=pslot: e.matmul(
                        psum[bank][pslot * 64:(pslot + 1) * 64, quad * 128:(quad + 1) * 128],
                        htb[:, h * 64:(h + 1) * 64], ce_m[:, hp * 128:(hp + 1) * 128],
                        start=False, stop=True), "pe" if hp == 15 else None)
                y_mm = S.val("pe")
                S.wait("tensor", "dv", xs2_done)
                S.wait("tensor", "dv", prev.get("dh_free", 0))
                for hp in range(16):
                    h = hh * 16 + hp
                    bank = 6 + hp // 8
                    S.op("tensor", lambda e, h=h, hp=hp, bank=bank: e.matmul(
                        psum[bank][:, (hp % 8) * 64:(hp % 8 + 1) * 64],
                        btm[:, :], xs2[:, hp * 64:(hp + 1) * 64], start=True, stop=True),
                        "pe" if hp == 15 else None)
                dh_mm = S.val("pe")
                S.wait("vector", "pe", y_mm)
                S.op("vector", lambda e, hh=hh, q=q: e.tensor_tensor(
                    tmp[:, 0:1024].rearrange("p (f i) -> p f i", i=128),
                    xssm[:, hh * 8:(hh + 1) * 8, q * 128:(q + 1) * 128],
                    dcm[:, hh * 8:(hh + 1) * 8].unsqueeze(2).broadcast_to([128, 8, 128]), ALU.mult), drain=True)
                for bk in range(2):
                    S.op("vector", lambda e, bk=bk, hh=hh: e.tensor_tensor(
                        y_sb[:, hh * 8 + bk * 4:hh * 8 + (bk + 1) * 4, :],
                        tmp[:, bk * 512:(bk + 1) * 512].rearrange("p (f i) -> p f i", i=128),
                        psum[4 + bk][:, :].rearrange("p (f i) -> p f i", i=128), ALU.add),
                        "dv" if bk == 1 else None)
                prev["y_free"] = S.val("dv")
                S.wait("vector", "pe", dh_mm)
                S.wait("vector", "dv", dec_done)
                S.op("vector", lambda e, hh=hh: e.tensor_tensor(
                    ht[:, hh * 1024:(hh + 1) * 1024].rearrange("p (h w) -> p h w", w=64),
                    ht[:, hh * 1024:(hh + 1) * 1024].rearrange("p (h w) -> p h w", w=64),
                    decays[:, hh * 16:(hh + 1) * 16].unsqueeze(2).broadcast_to([128, 16, 64]), ALU.mult), drain=True)
                for bk in range(2):
                    S.op("vector", lambda e, bk=bk, hh=hh: e.tensor_tensor(
                        ht[:, hh * 1024 + bk * 512:hh * 1024 + (bk + 1) * 512],
                        ht[:, hh * 1024 + bk * 512:hh * 1024 + (bk + 1) * 512],
                        psum[6 + bk][:, :], ALU.add), drain=True)
                prev["dh_free"] = S.op("vector", lambda e, hh=hh: e.tensor_copy(
                    htb[:, hh * 1024:(hh + 1) * 1024], ht[:, hh * 1024:(hh + 1) * 1024]), "dv")
            S.wait("scalar", "dv", prev["y_free"])
            ysq_done = S.op("scalar", lambda e: e.activation(
                tmp[:, :], y_sb[:, :, :].rearrange("p f i -> p (f i)"), AF.Square), "ac")
            S.wait("tensor", "ac", ysq_done)
            S.wait("tensor", "dv", prev["dh_free"])
            for fc in range(16):
                S.op("tensor", lambda e, fc=fc: e.matmul(
                    psum[6][0:1, 0:128], ones_c[:, :], tmp[:, fc * 128:(fc + 1) * 128],
                    start=(fc == 0), stop=(fc == 15)), "pe" if fc == 15 else None)
            S.wait("scalar", "pe", S.val("pe"))
            S.op("scalar", lambda e: e.activation(rrow, psum[6][0:1, 0:128], AF.Ln,
                                                  scale=1.0 / 2048.0, bias=epsc[0:1, 0:1]), drain=True)
            rms_done = S.op("scalar", lambda e: e.activation(rrow, rrow, AF.Exp, scale=-0.5), "ac", drain=True)
            S.wait("tensor", "ac", rms_done)
            S.op("tensor", lambda e: e.matmul(psum[7][:, 0:128], ones_r[:, :], rrow,
                                              start=True, stop=True), "pe")
            rstd_b = S.val("pe")
            S.wait("scalar", "dv", S.val("dv"))
            sz_done = S.op("scalar", lambda e, q=q: e.activation(
                sz[:, :, :], z_sb[:, :, q * 128:(q + 1) * 128], AF.Silu), "ac")
            S.wait("vector", "pe", rstd_b)
            S.op("vector", lambda e: e.tensor_tensor(
                dt_m[:, :].rearrange("p (f i) -> p f i", i=128), y_sb[:, :, :],
                nwcm[:, :].unsqueeze(2).broadcast_to([128, 16, 128]), ALU.mult), drain=True)
            S.op("vector", lambda e: e.tensor_tensor(
                dt_m[:, :].rearrange("p (f i) -> p f i", i=128),
                dt_m[:, :].rearrange("p (f i) -> p f i", i=128),
                psum[7][:, 0:128].unsqueeze(1).broadcast_to([128, 16, 128]), ALU.mult), "dv", drain=True)
            S.wait("vector", "ac", sz_done)
            S.wait("vector", "syg", prev.get("yg_stored", 0))
            yg_done = S.op("vector", lambda e, q=q: e.tensor_tensor(
                yg_sb[:, :, q * 128:(q + 1) * 128],
                dt_m[:, :].rearrange("p (f i) -> p f i", i=128), sz[:, :, :], ALU.mult), "dv")
            prev["b67_free"] = S.val("dv")
            prev["tmp_free"] = S.val("dv")
        S.wait("sync", "dv", S.val("dv"))
        prev["yg_stored"] = S.dma("sync", io["yg"][:, :, g0:g0 + T], yg_sb[:, :, :], "syg")
        for b in range(6):
            evac_done[("dv", b)] = S.val("dv")
        if ch == 7:
            S.wait("vector", "pe", S.val("pe"))
            S.op("vector", lambda e: e.memset(ht[:, :], 0.0))
            S.op("vector", lambda e: e.memset(htb[:, :], 0.0), "dv")
    S.wait("sync", "sxn", S.val("sxn"))
    S.wait("sync", "syg", S.val("syg"))
    S.wait("sync", "ss", S.val("ss"))



def build_phaseC(S, io, alloc, psum):
    q_fm = alloc("c_q", [128, 8, 2048], BF16)
    k_fm = alloc("c_k", [128, 8, 2048], BF16)
    vaug = alloc("c_v", [128, 16, 8 * 65 * 2], BF16)
    aout = alloc("c_ao", [128, 8, 2048], BF16)
    xn_sb = alloc("c_xn", [128, 8, 2048], BF16)
    w_sb = alloc("c_w", [128, 8, 1024], BF16)
    stexp = [alloc("c_se%d" % i, [128, 512], BF16) for i in range(2)]
    abias = alloc("c_ab", [128, 256], F32)
    cmask = alloc("c_cm", [128, 4, 512], BF16)
    rbrow = alloc("c_rb", [1, 512], F32)
    rbb = alloc("c_rbb", [64, 512], F32)
    ones_r = alloc("c_ones", [1, 64], F32)
    ev = alloc("c_ev", [128, 512], F32)

    S.op("vector", lambda e: e.memset(ones_r[:, :], 1.0), "dv")
    S.dma("sync", abias[:, :], io["abias"], "ld")
    ld_cm = S.dma("sync", cmask[:, :, :], io["cmask"].rearrange("d p t -> p d t"), "ld")
    S.wait("scalar", "ld", ld_cm)

    prev = {}
    for b in range(2):
        c0 = b * 2048
        S.wait("gpsimd", "pe", prev.get("xn_free", 0))
        ld_xn = S.dma("gpsimd", xn_sb[:, :, :], io["xn1"][:, :, c0:c0 + 2048], "cxn")
        S.wait("vector", "pe", prev.get("v_used", 0))
        vset = S.op("vector", lambda e: e.memset(vaug[:, :, :], 1.0), "dv")
        for wi, wname in enumerate(("wqT", "wkT", "wvT")):
            S.wait("gpsimd", "pe", prev.get("w_used", 0))
            ldw = S.dma("gpsimd", w_sb[:, :, :],
                        io[wname].rearrange("(c p) d -> p c d", p=128), "cw")
            S.wait("tensor", "cw", ldw)
            if wi == 0:
                S.wait("tensor", "cxn", ld_xn)
            dst = q_fm if wi == 0 else k_fm
            if wi < 2:
                for dc in range(8):
                    for tb4 in range(4):
                        bank = 6 + (dc * 4 + tb4) % 2
                        S.wait("tensor", "dv", prev.get(("qk", bank), 0))
                        for c in range(8):
                            S.op("tensor", lambda e, dc=dc, tb4=tb4, c=c, bank=bank: e.matmul(
                                psum[bank][:, :], w_sb[:, c, dc * 128:(dc + 1) * 128],
                                xn_sb[:, c, tb4 * 512:(tb4 + 1) * 512],
                                start=(c == 0), stop=(c == 7)), "pe" if c == 7 else None)
                        S.wait("vector", "pe", S.val("pe"))
                        prev[("qk", bank)] = S.op("vector", lambda e, dst=dst, dc=dc, tb4=tb4, bank=bank:
                            e.tensor_copy(dst[:, dc, tb4 * 512:(tb4 + 1) * 512], psum[bank][:, :]), "dv")
            else:
                S.wait("tensor", "dv", vset)
                for kt in range(16):
                    for db in range(2):
                        bank = 6 + (kt * 2 + db) % 2
                        S.wait("tensor", "dv", prev.get(("qk", bank), 0))
                        for c in range(8):
                            S.op("tensor", lambda e, kt=kt, db=db, c=c, bank=bank: e.matmul(
                                psum[bank][:, :], xn_sb[:, c, kt * 128:(kt + 1) * 128],
                                w_sb[:, c, db * 512:(db + 1) * 512],
                                start=(c == 0), stop=(c == 7)), "pe" if c == 7 else None)
                        S.wait("vector", "pe", S.val("pe"))
                        prev[("qk", bank)] = S.op("vector", lambda e, kt=kt, db=db, bank=bank:
                            e.tensor_copy(
                                vaug[:, kt, db * 520:db * 520 + 520].rearrange(
                                    "p (h w) -> p h w", w=65)[:, :, 0:64],
                                psum[bank][:, :].rearrange("p (h w) -> p h w", w=64)), "dv")
            w_used_v = S.val("pe")
            prev["w_used"] = w_used_v
        prev["xn_free"] = S.val("pe")
        qkv_done = S.val("dv")
        se_act = {}
        se_used = {}
        st_used = {}
        for h in range(16):
            hb = (h % 2) * 64
            hc = h // 2
            for qb in range(4):
                kts = 4 * qb + 4
                pvb = 2 + qb % 2
                S.wait("tensor", "dv", prev.get(("pv", pvb), 0))
                for kt in range(kts):
                    sb_ = kt % 2
                    S.wait("tensor", "ac", st_used.get(sb_, 0))
                    if h == 0 and qb == 0 and kt == 0:
                        S.wait("tensor", "dv", qkv_done)
                    S.op("tensor", lambda e, h=h, qb=qb, kt=kt, sb_=sb_, hb=hb, hc=hc: e.matmul(
                        psum[sb_][:, :], k_fm[hb:hb + 64, hc, kt * 128:(kt + 1) * 128],
                        q_fm[hb:hb + 64, hc, qb * 512:(qb + 1) * 512],
                        start=True, stop=True), "pe")
                    st_mm = S.val("pe")
                    S.wait("scalar", "pe", st_mm)
                    S.wait("scalar", "pe", se_used.get(sb_, 0))
                    st_used[sb_] = S.op("scalar", lambda e, h=h, kt=kt, sb_=sb_: e.activation(
                        stexp[sb_][:, :], psum[sb_][:, :], AF.Exp, scale=0.125,
                        bias=abias[:, h * 16 + kt:h * 16 + kt + 1]), "ac")
                    se_act[sb_] = st_used[sb_]
                    dg = kt - 4 * qb
                    if dg >= 0:
                        S.wait("vector", "ac", se_act[sb_])
                        mk_done = S.op("vector", lambda e, dg=dg, sb_=sb_: e.tensor_mul(
                            stexp[sb_][:, :], stexp[sb_][:, :], cmask[:, dg, :]), "dv")
                        S.wait("tensor", "dv", mk_done)
                    else:
                        S.wait("tensor", "ac", st_used[sb_])
                    S.op("tensor", lambda e, h=h, kt=kt, sb_=sb_, kts=kts, pvb=pvb: e.matmul(
                        psum[pvb][0:65, :], vaug[:, kt, h * 65:h * 65 + 65], stexp[sb_][:, :],
                        start=(kt == 0), stop=(kt == kts - 1)), "pe")
                    se_used[sb_] = S.val("pe")
                pv_mm = S.val("pe")
                S.wait("vector", "pe", pv_mm)
                rc = S.op("vector", lambda e, pvb=pvb: e.reciprocal(rbrow[:, :], psum[pvb][64:65, :]), "dv", drain=True)
                S.wait("tensor", "dv", rc)
                bcb = 4 + qb % 2
                bc = S.op("tensor", lambda e, bcb=bcb: e.matmul(
                    psum[bcb][0:64, :], ones_r[:, :], rbrow[0:1, :],
                    start=True, stop=True), "pe")
                S.wait("vector", "pe", bc)
                S.op("vector", lambda e, bcb=bcb: e.tensor_copy(rbb[:, :], psum[bcb][0:64, :]), drain=True)
                prev[("pv", pvb)] = S.op("vector", lambda e, h=h, qb=qb, pvb=pvb, hb=hb, hc=hc:
                    e.tensor_mul(aout[hb:hb + 64, hc, qb * 512:(qb + 1) * 512],
                                 psum[pvb][0:64, :], rbb[:, :]), "dv")
        prev["v_used"] = S.val("pe")
        S.wait("gpsimd", "pe", prev.get("w_used", 0))
        ldw = S.dma("gpsimd", w_sb[:, :, :], io["woT"].rearrange("(c p) d -> p c d", p=128), "cw")
        S.wait("tensor", "cw", ldw)
        S.wait("tensor", "dv", S.val("dv"))
        for dc in range(8):
            for tb4 in range(4):
                bank = 6 + (dc * 4 + tb4) % 2
                S.wait("tensor", "dv", prev.get(("s1ev", bank), 0))
                for c in range(8):
                    S.op("tensor", lambda e, dc=dc, tb4=tb4, c=c, bank=bank: e.matmul(
                        psum[bank][:, :], w_sb[:, c, dc * 128:(dc + 1) * 128],
                        aout[:, c, tb4 * 512:(tb4 + 1) * 512],
                        start=(c == 0), stop=(c == 7)), "pe" if c == 7 else None)
                S.wait("vector", "pe", S.val("pe"))
                S.wait("vector", "st", prev.get("ev_st", 0))
                evd = S.op("vector", lambda e, bank=bank: e.tensor_copy(ev[:, :], psum[bank][:, :]), "dv")
                S.wait("sync", "dv", evd)
                prev["ev_st"] = S.dma("sync", io["s1"][:, dc, c0 + tb4 * 512:c0 + (tb4 + 1) * 512],
                                      ev[:, :], "st")
                prev[("s1ev", bank)] = evd
        prev["w_used"] = S.val("pe")
        prev[("qk", 6)] = prev[("s1ev", 6)]
        prev[("qk", 7)] = prev[("s1ev", 7)]
    S.wait("sync", "st", S.val("st"))



def build_phaseD(S, io, alloc, psum):
    TB = 1024
    xn_sb = alloc("d_xn", [128, 8, TB], BF16)
    h_sb = alloc("d_h", [128, 32 * TB], BF16)
    acc = alloc("d_acc", [128, 8, TB], F32)
    wb = alloc("d_wb", [128, TB], F32)
    w1s = [alloc("d_w1%d" % i, [128, 8, 256], BF16) for i in range(2)]
    w2s = [alloc("d_w2%d" % i, [128, 32, 128], BF16) for i in range(2)]
    sil = [alloc("d_sil%d" % i, [128, 512], BF16) for i in range(4)]
    tmp2 = alloc("d_tmp2", [128, TB], F32)
    ev = alloc("d_ev", [128, TB], F32)
    b1g = alloc("d_b1g", [128, 8, 32], F32)
    b1u = alloc("d_b1u", [128, 8, 32], F32)
    b2a = alloc("d_b2", [128, 8, 8], F32)
    idf = alloc("d_idf", [128, 128], F32)
    xr = [alloc("d_xr%d" % i, [128, 8, 128], F32) for i in range(2)]
    rw_sb = alloc("d_rw", [128, 8, 8], F32)
    rb_sb = alloc("d_rb", [1, 8], F32)
    onr = alloc("d_onr", [1, 128], F32)
    lt = alloc("d_lt", [128, 8], F32)
    ltw = alloc("d_ltw", [128, 8], F32)
    mrow = alloc("d_m", [128, 4], F32)
    wn_fm = alloc("d_wnfm", [8, TOK], F32)
    wn_d = S.nc.dram_tensor("d_wn_d", [8, TOK], F32, kind="Internal").ap()

    onr_done = S.op("vector", lambda e: e.memset(onr[:, :], 1.0), "dv")
    S.dma("sync", b1g[:, :, :], io["b1g"].rearrange("e p k -> p e k"), "din")
    S.dma("sync", b1u[:, :, :], io["b1u"].rearrange("e p k -> p e k"), "din")
    S.dma("sync", b2a[:, :, :], io["b2r"].rearrange("e p k -> p e k"), "din")
    S.dma("sync", idf[:, :], io["identf"], "din")
    S.dma("sync", rw_sb[:, :, :], io["rwT"].rearrange("(c p) e -> p c e", p=128), "din")
    ld_init = S.dma("sync", rb_sb[:, :], io["rb"], "din")

    xr_used = [0, 0]
    wn_done = {}
    for tt_ in range(32):
        ib = tt_ % 2
        S.wait("sync", "pe", xr_used[ib])
        ldx = S.dma("sync", xr[ib][:, :, :], io["xn1"][:, :, tt_ * 128:(tt_ + 1) * 128], "xr%d" % ib)
        bank = tt_ % 2
        S.wait("tensor", "xr%d" % ib, ldx)
        if tt_ < 2:
            S.wait("tensor", "din", ld_init)
            S.wait("tensor", "dv", onr_done)
        S.wait("tensor", "dv", wn_done.get(tt_ - 2, 0))
        for c in range(8):
            S.op("tensor", lambda e, c=c, ib=ib, bank=bank: e.matmul(
                psum[bank][0:128, 0:8], xr[ib][:, c, :], rw_sb[:, c, :],
                start=(c == 0), stop=False))
        S.op("tensor", lambda e, ib=ib, bank=bank: e.matmul(
            psum[bank][0:128, 0:8], onr[:, :], rb_sb[:, :], start=False, stop=True), "pe")
        xr_used[ib] = S.val("pe")
        lg_mm = S.val("pe")
        S.wait("vector", "pe", lg_mm)
        S.op("vector", lambda e, bank=bank: e.tensor_copy(lt[:, :], psum[bank][0:128, 0:8]), drain=True)
        S.op("vector", lambda e: e.reduce_max(mrow[:, 0:1], lt[:, :], AX.X), drain=True)
        S.op("vector", lambda e: e.tensor_scalar(ltw[:, :], lt[:, :], mrow[:, 0:1], None, ALU.is_equal),
             drain=True)
        S.op("vector", lambda e: e.scalar_tensor_tensor(ltw[:, :], ltw[:, :], -1e9, lt[:, :],
                                                        ALU.mult, ALU.add), drain=True)
        S.op("vector", lambda e: e.reduce_max(mrow[:, 1:2], ltw[:, :], AX.X), drain=True)
        S.op("vector", lambda e: e.tensor_scalar(ltw[:, :], lt[:, :], mrow[:, 1:2], None, ALU.is_ge),
             drain=True)
        S.op("vector", lambda e: e.tensor_scalar(mrow[:, 2:3], mrow[:, 0:1], -1.0, None, ALU.mult),
             "dv", drain=True)
        S.wait("scalar", "dv", S.val("dv"))
        S.op("scalar", lambda e: e.activation(lt[:, :], lt[:, :], AF.Exp, bias=mrow[:, 2:3]),
             "ac", drain=True)
        S.wait("vector", "ac", S.val("ac"))
        S.op("vector", lambda e: e.tensor_mul(lt[:, :], lt[:, :], ltw[:, :]), drain=True)
        S.op("vector", lambda e: e.reduce_sum(mrow[:, 3:4], lt[:, :], AX.X), drain=True)
        S.op("vector", lambda e: e.reciprocal(mrow[:, 2:3], mrow[:, 3:4]), drain=True)
        tw_done = S.op("vector", lambda e: e.tensor_scalar(
            lt[:, :], lt[:, :], mrow[:, 2:3], None, ALU.mult), "dv", drain=True)
        S.wait("tensor", "dv", tw_done)
        S.op("tensor", lambda e, bank=bank: e.transpose(
            psum[2 + bank][0:8, 0:128], lt[:, :], idf[:, :]), "pe")
        tr_mm = S.val("pe")
        S.wait("vector", "pe", tr_mm)
        wn_done[tt_] = S.op("vector", lambda e, tt_=tt_, bank=bank: e.tensor_copy(
            wn_fm[:, tt_ * 128:(tt_ + 1) * 128], psum[2 + bank][0:8, 0:128]), "dv", drain=True)
    S.wait("sync", "dv", wn_done[31])
    wn_st = S.dma("sync", wn_d, wn_fm[:, :], "swn")

    prev = {}
    for tb in range(4):
        t0 = tb * TB
        S.wait("gpsimd", "pe", prev.get("xn_used", 0))
        ld_xn = S.dma("gpsimd", xn_sb[:, :, :], io["xn1"][:, :, t0:t0 + TB], "dxn")
        for e_ in range(8):
            S.wait("sync", "swn", wn_st)
            S.wait("sync", "dv", prev.get("wb_used", 0))
            ld_wb = S.dma("sync", wb[:, :].unsqueeze(1),
                          wn_d[e_, t0:t0 + TB].unsqueeze(0).partition_broadcast(128), "lwb")
            sil_done = {}
            h_done = {}
            for kc in range(32):
                ib = kc % 2
                S.wait("sync", "pe", prev.get(("w1", ib), 0))
                l1 = S.dma("sync", w1s[ib][:, :, 0:128],
                           io["w1b"][e_].rearrange("(c p) h -> p c h", p=128)[:, :, kc * 128:(kc + 1) * 128],
                           "lw1%d" % ib)
                l2 = S.dma("sync", w1s[ib][:, :, 128:256],
                           io["w1b"][e_].rearrange("(c p) h -> p c h", p=128)[:, :, 4096 + kc * 128:4096 + (kc + 1) * 128],
                           "lw1%d" % ib)
                S.wait("tensor", "lw1%d" % ib, l2)
                if kc == 0 and e_ == 0:
                    S.wait("tensor", "dxn", ld_xn)
                gb = [psum[0], psum[1]] if ib == 0 else [psum[4], psum[5]]
                ub = [psum[2], psum[3]] if ib == 0 else [psum[6], psum[7]]
                if kc >= 2:
                    S.wait("tensor", "ac", sil_done[kc - 2])
                    S.wait("tensor", "dv", h_done[kc - 2])
                else:
                    S.wait("tensor", "ac", prev.get("dn_read", 0))
                    S.wait("tensor", "dv", prev.get("dn_read_dv", 0))
                for nb in range(2):
                    for c in range(8):
                        S.op("tensor", lambda e, c=c, nb=nb, ib=ib, gb=gb: e.matmul(
                            gb[nb][:, :], w1s[ib][:, c, 0:128],
                            xn_sb[:, c, nb * 512:(nb + 1) * 512],
                            start=(c == 0), stop=(c == 7)))
                for nb in range(2):
                    for c in range(8):
                        S.op("tensor", lambda e, c=c, nb=nb, ib=ib, ub=ub: e.matmul(
                            ub[nb][:, :], w1s[ib][:, c, 128:256],
                            xn_sb[:, c, nb * 512:(nb + 1) * 512],
                            start=(c == 0), stop=(c == 7)), "pe" if (c == 7 and nb == 1) else None)
                prev[("w1", ib)] = S.val("pe")
                S.wait("scalar", "pe", S.val("pe"))
                for nb in range(2):
                    sil_done[kc] = S.op("scalar", lambda e, nb=nb, ib=ib, e_=e_, kc=kc, gb=gb: e.activation(
                        sil[2 * ib + nb][:, :], gb[nb][:, :], AF.Silu,
                        bias=b1g[:, e_, kc:kc + 1]), "ac")
                S.wait("vector", "ac", sil_done[kc])
                for nb in range(2):
                    h_done[kc] = S.op("vector", lambda e, nb=nb, ib=ib, e_=e_, kc=kc, ub=ub: e.scalar_tensor_tensor(
                        h_sb[:, kc * TB + nb * 512:kc * TB + (nb + 1) * 512],
                        ub[nb][:, :], b1u[:, e_, kc:kc + 1],
                        sil[2 * ib + nb][:, :], ALU.add, ALU.mult), "dv")
            S.wait("vector", "lwb", ld_wb)
            hview = h_sb.ap().rearrange("p (c t) -> p c t", t=TB)
            dn_ac = {}
            dn_dv = {}
            for dc in range(8):
                ib = dc % 2
                pd = [psum[0], psum[1]] if ib == 0 else [psum[2], psum[3]]
                S.wait("sync", "pe", prev.get(("w2", ib), 0))
                lw2 = S.dma("sync", w2s[ib][:, :, :],
                            io["w2b"][e_].rearrange("(c p) d -> p c d", p=128)[:, :, dc * 128:(dc + 1) * 128],
                            "lw2%d" % ib)
                S.wait("tensor", "lw2%d" % ib, lw2)
                if dc == 0:
                    S.wait("tensor", "dv", h_done[31])
                    S.wait("tensor", "ac", S.val("ac"))
                elif dc == 1:
                    S.wait("tensor", "dv", h_done[30])
                else:
                    S.wait("tensor", "dv", dn_dv[dc - 2])
                for nb in range(2):
                    for c in range(32):
                        S.op("tensor", lambda e, c=c, nb=nb, ib=ib, pd=pd: e.matmul(
                            pd[nb][:, :], w2s[ib][:, c, :], hview[:, c, nb * 512:(nb + 1) * 512],
                            start=(c == 0), stop=(c == 31)), "pe" if (c == 31 and nb == 1) else None)
                prev[("w2", ib)] = S.val("pe")
                S.wait("vector", "pe", S.val("pe"))
                S.wait("vector", "dv", prev.get("tmp2_rd", 0))
                for nb in range(2):
                    sl = slice(nb * 512, (nb + 1) * 512)
                    S.op("vector", lambda e, sl=sl, nb=nb, pd=pd, e_=e_, dc=dc: e.scalar_tensor_tensor(
                        tmp2[:, sl], pd[nb][:, :], b2a[:, e_, dc:dc + 1], wb[:, sl],
                        ALU.add, ALU.mult), "dv", drain=(nb == 1))
                if e_ == 0:
                    dn_dv[dc] = S.op("vector", lambda e, dc=dc: e.tensor_copy(
                        acc[:, dc, :], tmp2[:, :]), "dv")
                else:
                    S.wait("vector", "dv", prev.get(("acc", dc), 0))
                    dn_dv[dc] = S.op("vector", lambda e, dc=dc: e.tensor_add(
                        acc[:, dc, :], acc[:, dc, :], tmp2[:, :]), "dv")
                prev[("acc", dc)] = dn_dv[dc]
                prev["tmp2_rd"] = dn_dv[dc]
                prev[("dn_last", ib)] = dn_dv[dc]
            prev["dn_read"] = S.val("ac")
            prev["dn_read_dv"] = S.val("dv")
            prev["wb_used"] = S.val("dv")
        prev["xn_used"] = S.val("pe")
        hb32 = h_sb.ap().bitcast(F32)
        x1v = hb32[:, 0:8 * TB].rearrange("p (c t) -> p c t", t=TB)
        s1v = hb32[:, 8 * TB:16 * TB].rearrange("p (c t) -> p c t", t=TB)
        S.wait("sync", "pe", S.val("pe"))
        l_x1 = S.dma("sync", x1v, io["x1"][:, :, t0:t0 + TB], "lxe")
        l_s1 = S.dma("sync", s1v, io["s1"][:, :, t0:t0 + TB], "lse")
        for dc in range(8):
            S.wait("scalar", "dv", S.val("dv"))
            sg_d = S.op("scalar", lambda e, dc=dc: e.activation(
                tmp2[:, :], acc[:, dc, :], AF.Silu), "ac")
            S.wait("vector", "ac", sg_d)
            if dc == 0:
                S.wait("vector", "lse", l_s1)
                S.wait("vector", "lxe", l_x1)
            S.wait("vector", "so", prev.get("o_st", 0))
            S.op("vector", lambda e, dc=dc: e.tensor_mul(tmp2[:, :], s1v[:, dc, :], tmp2[:, :]), drain=True)
            od = S.op("vector", lambda e, dc=dc: e.tensor_add(ev[:, :], x1v[:, dc, :], tmp2[:, :]), "dv")
            S.wait("sync", "dv", od)
            prev["o_st"] = S.dma("sync", io["out_fm"][:, dc, t0:t0 + TB], ev[:, :], "so")
    S.wait("sync", "so", S.val("so"))


BUILDERS = {"P": build_phaseP, "A": build_phaseA, "B": build_phaseB, "C": build_phaseC, "D": build_phaseD}


def run_phase(phase, state_list, n_cores=None):
    nc = bass.Bass("TRN2", target_bir_lowering=False)
    ins, outs = PHASE_IO[phase]
    kinds = {n: "ExternalInput" for n in ins}
    kinds.update({n: "ExternalOutput" for n in outs})
    io = {}
    for n in ins + outs:
        shape, dt = TENSORS[n]
        io[n] = nc.dram_tensor(n, shape, dt, kind=kinds[n]).ap()
    S = Sched(nc)

    def alloc(name, shape, dt):
        return nc.alloc_sbuf_tensor(name, shape, dt)

    psum = [nc.alloc_psum_tensor("ps%d" % i, [128, 512], F32) for i in range(8)]
    BUILDERS[phase](S, io, alloc, psum)
    S.emit()
    n = len(state_list) if n_cores is None else n_cores
    in_maps = [{k: state_list[c][k] for k in ins} for c in range(n)]
    res = run_bass_kernel_spmd(nc, in_maps, core_ids=list(range(n)))
    for c in range(n):
        for k in outs:
            state_list[c][k] = res.results[c][k]
    return res


def kernel(**inputs):
    shards, w = host_prep(inputs)
    state = []
    for c in range(NCORES):
        st = dict(w)
        st["x_fm"] = shards[c]
        state.append(st)
    for ph in ["P", "A", "B", "C", "D"]:
        run_phase(ph, state)
    outs = []
    for c in range(NCORES):
        o = state[c]["out_fm"]
        o = np.asarray(o).transpose(1, 0, 2).reshape(1024, TOK)
        outs.append(np.stack([o[:, :SEQ].T, o[:, SEQ:].T]))
    return np.concatenate(outs, axis=0).astype(np.float32)


# version 04 (final)
# speedup vs baseline: 1.0543x; 1.0543x over previous
import numpy as np
import ml_dtypes
import concourse.bass as bass
import concourse.mybir as mybir
from concourse.bass_utils import run_bass_kernel_spmd

F32 = mybir.dt.float32
BF16 = mybir.dt.bfloat16
I32 = mybir.dt.int32
AF = mybir.ActivationFunctionType
ALU = mybir.AluOpType
AX = mybir.AxisListType
bfdt = ml_dtypes.bfloat16

NCORES = 8
TOK = 4096
SEQ = 2048
NH, HD, DS = 32, 64, 128
DI = 2048
SHIFT = 20.0
EPS = 1e-5

ENGINES = ("tensor", "vector", "scalar", "gpsimd", "sync")


class Sched:

    def __init__(self, nc):
        self.nc = nc
        self.q = {e: [] for e in ENGINES}
        self.sems = {}

    def sem(self, name):
        if name not in self.sems:
            self.sems[name] = [self.nc.alloc_semaphore(name=name), 0]
        return self.sems[name]

    def val(self, name):
        return self.sem(name)[1]

    def on(self, eng, fn):
        self.q[eng].append(fn)

    def wait(self, eng, name, v=None):
        s = self.sem(name)
        if v is None:
            v = s[1]
        if v is None or v <= 0:
            return
        self.on(eng, lambda e, h=s[0], vv=v: e.wait_ge(h, vv))

    def op(self, eng, fn, sem=None, n=1, drain=False):
        if sem is None:
            self.on(eng, fn)
            if drain:
                self.on(eng, lambda e: e.drain())
            return None
        s = self.sem(sem)
        s[1] += n
        if drain:
            self.on(eng, lambda e, f=fn: f(e))
            self.on(eng, lambda e, h=s[0], nn=n: e.drain().then_inc(h, nn))
        else:
            self.on(eng, lambda e, f=fn, h=s[0], nn=n: f(e).then_inc(h, nn))
        return s[1]

    def dma(self, eng, out, in_, sem=None, **kw):
        return self.op(eng, lambda e, o=out, i=in_: e.dma_start(o, i, **kw), sem, 16)

    def emit(self):
        with self.nc.Block() as block:
            for e in ENGINES:
                def run(eng_obj, items=tuple(self.q[e])):
                    for f in items:
                        f(eng_obj)
                getattr(block, e)(run)
        for e in ENGINES:
            self.q[e] = []


TENSORS = {
    "x_fm": ([128, 8, TOK], F32),
    "out_fm": ([128, 8, TOK], F32),
    "winT": ([1024, 4384], F32),
    "convw": ([18, 128, 4], F32),
    "convb": ([128, 18], F32),
    "dtb": ([32, 1], F32),
    "alog": ([32, 1], F32),
    "dcm": ([128, 16], F32),
    "nwcm": ([128, 16], F32),
    "woutT": ([DI, 1024], F32),
    "wgT": ([1024, 3072], I32),
    "wuT": ([1024, 3072], I32),
    "wdT": ([3072, 1024], I32),
    "wqT": ([1024, 1024], F32),
    "wkT": ([1024, 1024], F32),
    "wvT": ([1024, 1024], F32),
    "woT": ([1024, 1024], F32),
    "rwT": ([1024, 8], F32),
    "rb": ([1, 8], F32),
    "w1T": ([8, 1024, 8192], F32),
    "b1g": ([8, 128, 32], F32),
    "b1u": ([8, 128, 32], F32),
    "w2T": ([8, 4096, 1024], F32),
    "b2r": ([8, 128, 8], F32),
    "identf": ([128, 128], F32),
    "identb": ([128, 128], BF16),
    "trimask": ([128, 128], BF16),
    "cmask": ([4, 128, 512], BF16),
    "abias": ([128, 256], F32),
    "xn0": ([128, 8, TOK], BF16),
    "yg": ([128, 16, TOK], BF16),
    "x1": ([128, 8, TOK], F32),
    "xn1": ([128, 8, TOK], F32),
    "s1": ([128, 8, TOK], F32),
    "w1b": ([8, 1024, 8192], BF16),
    "w2b": ([8, 4096, 1024], BF16),
}

PHASE_IO = {
    "P": (["w1T", "w2T"], ["w1b", "w2b"]),
    "A": (["x_fm", "winT", "convw", "convb", "dtb", "alog", "dcm", "nwcm",
           "identf", "identb", "trimask"], ["xn0", "yg"]),
    "B": (["x_fm", "xn0", "yg", "woutT", "wgT", "wuT", "wdT"], ["x1", "xn1"]),
    "C": (["xn1", "wqT", "wkT", "wvT", "woT", "abias", "cmask"], ["s1"]),
    "D": (["xn1", "x1", "s1", "rwT", "rb", "w1T", "w2T", "b1g", "b1u", "b2r",
           "identf"], ["out_fm"]),
}


def host_prep(inputs):
    f32 = np.float32
    x = inputs["L0_attn_x"]
    shards = []
    for c in range(NCORES):
        xc = np.concatenate([x[2 * c].T, x[2 * c + 1].T], axis=1)
        shards.append(np.ascontiguousarray(xc.reshape(8, 128, TOK).transpose(1, 0, 2)))
    w = {}
    w["winT"] = np.ascontiguousarray(inputs["L0_attn_Win"].T)
    w["convw"] = np.ascontiguousarray(inputs["L0_attn_conv_w"].reshape(18, 128, 4))
    w["convb"] = np.ascontiguousarray(inputs["L0_attn_conv_b"].reshape(18, 128).T)
    w["dtb"] = np.ascontiguousarray(inputs["L0_attn_dt_bias"][:, None])
    w["alog"] = np.ascontiguousarray(inputs["L0_attn_A_log"][:, None])
    D = inputs["L0_attn_D"]
    w["dcm"] = np.ascontiguousarray(
        D[(2 * np.arange(16)[None, :] + (np.arange(128)[:, None] >= 64)).astype(int)]).astype(f32)
    w["nwcm"] = np.ascontiguousarray(inputs["L0_attn_norm_w"].reshape(16, 128).T)
    w["woutT"] = np.ascontiguousarray(inputs["L0_attn_Wout"].T)
    w["wgT"] = np.ascontiguousarray(inputs["L0_ffn_w_gate"].T)
    w["wuT"] = np.ascontiguousarray(inputs["L0_ffn_w_up"].T)
    w["wdT"] = np.ascontiguousarray(inputs["L0_ffn_w_down"].T)
    w["wqT"] = np.ascontiguousarray(inputs["L1_attn_Wq"].T)
    w["wkT"] = np.ascontiguousarray(inputs["L1_attn_Wk"].T)
    w["wvT"] = np.ascontiguousarray(inputs["L1_attn_Wv"].T)
    w["woT"] = np.ascontiguousarray(inputs["L1_attn_Wo"].T)
    w["rwT"] = np.ascontiguousarray(inputs["L1_ffn_router_w"].T)
    w["rb"] = np.ascontiguousarray(inputs["L1_ffn_router_b"][None, :])
    w["w1T"] = np.ascontiguousarray(inputs["L1_ffn_w1"].transpose(0, 2, 1))
    b1 = inputs["L1_ffn_b1"]
    w["b1g"] = np.ascontiguousarray(b1[:, :4096].reshape(8, 32, 128).transpose(0, 2, 1))
    w["b1u"] = np.ascontiguousarray(b1[:, 4096:].reshape(8, 32, 128).transpose(0, 2, 1))
    w["w2T"] = np.ascontiguousarray(inputs["L1_ffn_w2"].transpose(0, 2, 1))
    w["b2r"] = np.ascontiguousarray(inputs["L1_ffn_b2"].reshape(8, 8, 128).transpose(0, 2, 1))
    w["identf"] = np.eye(128, dtype=f32)
    w["identb"] = np.eye(128, dtype=f32).astype(bfdt)
    jj = np.arange(128)
    w["trimask"] = (jj[None, :] >= jj[:, None]).astype(f32).astype(bfdt)
    cm = np.zeros((4, 128, 512), np.float32)
    for d in range(4):
        cm[d] = (np.arange(512)[None, :] >= (jj[:, None] + d * 128)).astype(np.float32)
    w["cmask"] = cm.astype(bfdt)
    slopes = np.array([1.0 / ((2.0 ** 8) ** ((i + 1.0) / 16.0)) for i in range(16)], np.float64)
    ab = np.zeros((128, 256), np.float64)
    for h in range(16):
        for kt in range(16):
            ab[:, h * 16 + kt] = -slopes[h] * (kt * 128 + jj) - SHIFT
    w["abias"] = ab.astype(f32)
    return shards, w


def build_phaseP(S, io, alloc, psum):
    stg = [alloc("pstg%d" % i, [128, 8192], BF16) for i in range(2)]
    last_store = [0, 0]
    jobs = []
    for e in range(8):
        for kc in range(8):
            jobs.append((io["w1T"][e, kc * 128:(kc + 1) * 128, :],
                         io["w1b"][e, kc * 128:(kc + 1) * 128, :], False))
    for e in range(8):
        for half in range(4):
            src = io["w2T"][e].rearrange("(c p) d -> p c d", p=128)[:, half * 8:(half + 1) * 8, :]
            dst = io["w2b"][e].rearrange("(c p) d -> p c d", p=128)[:, half * 8:(half + 1) * 8, :]
            jobs.append((src, dst, True))
    for idx, (src, dst, three_d) in enumerate(jobs):
        i = idx % 2
        sview = stg[i].ap().rearrange("p (c d) -> p c d", d=1024) if three_d else stg[i][:, :]
        S.wait("gpsimd", "pst%d" % i, last_store[i])
        lv = S.dma("gpsimd", sview, src, "pld%d" % i)
        S.wait("sync", "pld%d" % i, lv)
        last_store[i] = S.dma("sync", dst, sview, "pst%d" % i)
    S.wait("sync", "pst0", S.val("pst0"))
    S.wait("sync", "pst1", S.val("pst1"))


def build_phaseB(S, io, alloc, psum):
    TB = 1024
    woutT = alloc("b_woutT", [128, 16, 1024], BF16)
    arena = alloc("b_arena", [128, 16 * 1024], BF16)
    h_sb = alloc("b_h", [128, 24 * 1024], BF16)
    sg_sb = alloc("b_sg", [128, 8 * 1024], BF16)
    x1_sb = alloc("b_x1", [128, 8 * 1024], F32)
    sil = [alloc("b_sil%d" % i, [128, 1024], BF16) for i in range(2)]
    wg_st = [alloc("b_wg%d" % i, [128, 8, 128], BF16) for i in range(2)]
    wu_st = [alloc("b_wu%d" % i, [128, 8, 128], BF16) for i in range(2)]
    wd_st = [alloc("b_wd%d" % i, [128, 24, 128], BF16) for i in range(2)]
    x_sb = [alloc("b_xin%d" % i, [128, 1024], F32) for i in range(2)]
    rows = alloc("b_rows", [1, 2 * 1024], F32)
    ones_f = alloc("b_ones", [128, 1], F32)
    rowt = alloc("b_rowt", [1, 1024], F32)
    ones_r = alloc("b_onesr", [1, 128], F32)
    epsc = alloc("b_eps", [1, 1], F32)
    tmp = h_sb.ap().bitcast(F32)

    S.op("vector", lambda e: e.memset(ones_f[:, :], 1.0))
    S.op("vector", lambda e: e.memset(ones_r[:, :], 1.0))
    S.op("vector", lambda e: e.memset(epsc[:, :], EPS), "dv")
    wout_ready = S.dma("gpsimd", woutT[:, :, :], io["woutT"].rearrange("(c p) d -> p c d", p=128), "wo")

    wgTv = io["wgT"].rearrange("(c p) h -> p c h", p=128)
    wuTv = io["wuT"].rearrange("(c p) h -> p c h", p=128)
    wdTv = io["wdT"].rearrange("(c p) d -> p c d", p=128)

    arena_free = 0
    psg_read = 0
    psd_read = 0
    arena_free = 0
    x_used = {}
    wbuf_used = [0, 0]
    dbuf_used = [0, 0]
    prev_s1x = 0

    for tb in range(4):
        t0 = tb * TB
        xn0v = arena.ap().rearrange("p (c t) -> p c t", t=1024)[:, 0:8, :]
        S.wait("sync", "sx", arena_free)
        ld_xn = S.dma("sync", xn0v, io["xn0"][:, :, t0:t0 + TB], "lxn")
        sil_done = {}
        h_done = {}
        for kc in range(24):
            ib = kc % 2
            gb = [psum[0], psum[1]] if ib == 0 else [psum[4], psum[5]]
            ub = [psum[2], psum[3]] if ib == 0 else [psum[6], psum[7]]
            S.wait("gpsimd", "pe", wbuf_used[ib])
            lg = S.dma("gpsimd", wg_st[ib][:, :, :], wgTv[:, :, kc * 128:(kc + 1) * 128], "w%d" % ib)
            lu = S.dma("gpsimd", wu_st[ib][:, :, :], wuTv[:, :, kc * 128:(kc + 1) * 128], "w%d" % ib)
            S.wait("tensor", "w%d" % ib, lu)
            if kc == 0:
                S.wait("tensor", "lxn", ld_xn)
            if kc >= 2:
                S.wait("tensor", "ac", sil_done[kc - 2])
                S.wait("tensor", "dv", h_done[kc - 2])
            elif tb > 0:
                S.wait("tensor", "dv", psg_read if ib == 0 else psd_read)
            for nb in range(2):
                for c in range(8):
                    S.op("tensor", lambda e, c=c, nb=nb, ib=ib, gb=gb: e.matmul(
                        gb[nb][:, :], wg_st[ib][:, c, :], xn0v[:, c, nb * 512:(nb + 1) * 512],
                        start=(c == 0), stop=(c == 7)))
            for nb in range(2):
                for c in range(8):
                    S.op("tensor", lambda e, c=c, nb=nb, ib=ib, ub=ub: e.matmul(
                        ub[nb][:, :], wu_st[ib][:, c, :], xn0v[:, c, nb * 512:(nb + 1) * 512],
                        start=(c == 0), stop=(c == 7)), "pe" if (c == 7 and nb == 1) else None)
            wbuf_used[ib] = S.val("pe")
            S.wait("scalar", "pe", S.val("pe"))
            for nb in range(2):
                sil_done[kc] = S.op("scalar", lambda e, nb=nb, ib=ib, gb=gb: e.activation(
                    sil[ib][:, nb * 512:(nb + 1) * 512], gb[nb][:, :], AF.Silu, scale=1.0 / 127.0), "ac")
            S.wait("vector", "ac", sil_done[kc])
            if kc == 0:
                S.wait("vector", "sx", arena_free if tb > 0 else 0)
            hv = h_sb[:, kc * 1024:(kc + 1) * 1024]
            for nb in range(2):
                h_done[kc] = S.op("vector", lambda e, nb=nb, ib=ib, ub=ub, hv=hv: e.scalar_tensor_tensor(
                    hv[:, nb * 512:(nb + 1) * 512], ub[nb][:, :], 1.0 / 127.0,
                    sil[ib][:, nb * 512:(nb + 1) * 512], ALU.mult, ALU.mult), "dv")
        S.wait("sync", "pe", S.val("pe"))
        ygv = arena.ap().rearrange("p (c t) -> p c t", t=1024)
        ld_yg = S.dma("sync", ygv, io["yg"][:, :, t0:t0 + TB], "lyg")
        hview = h_sb.ap().rearrange("p (c t) -> p c t", t=1024)
        dn_done = {}
        for dc in range(8):
            ib = dc % 2
            pd = [psum[0], psum[1]] if ib == 0 else [psum[2], psum[3]]
            S.wait("gpsimd", "pe", dbuf_used[ib])
            ldw = S.dma("gpsimd", wd_st[ib][:, :, :], wdTv[:, :, dc * 128:(dc + 1) * 128], "wd%d" % ib)
            S.wait("tensor", "wd%d" % ib, ldw)
            if dc == 0:
                S.wait("tensor", "dv", h_done[23])
                S.wait("tensor", "ac", sil_done[22])
            elif dc == 1:
                S.wait("tensor", "dv", h_done[22])
            else:
                S.wait("tensor", "ac", dn_done[dc - 2])
            for nb in range(2):
                for c in range(24):
                    S.op("tensor", lambda e, c=c, nb=nb, ib=ib, pd=pd: e.matmul(
                        pd[nb][:, :], wd_st[ib][:, c, :], hview[:, c, nb * 512:(nb + 1) * 512],
                        start=(c == 0), stop=(c == 23)), "pe" if (c == 23 and nb == 1) else None)
            dbuf_used[ib] = S.val("pe")
            S.wait("scalar", "pe", S.val("pe"))
            for nb in range(2):
                dn_done[dc] = S.op("scalar", lambda e, dc=dc, nb=nb, pd=pd: e.activation(
                    sg_sb[:, dc * 1024 + nb * 512:dc * 1024 + (nb + 1) * 512],
                    pd[nb][:, :], AF.Silu, scale=1.0 / 127.0), "ac")
        S.wait("tensor", "wo", wout_ready)
        S.wait("tensor", "lyg", ld_yg)
        x1done = {}
        ld_x = {}
        for dc in range(8):
            ib = dc % 2
            pw = [psum[4], psum[5]] if ib == 0 else [psum[6], psum[7]]
            S.wait("sync", "dv", x_used.get(dc - 2, 0))
            ld_x[dc] = S.dma("sync", x_sb[ib][:, :], io["x_fm"][:, dc, t0:t0 + TB], "lx%d" % ib)
            if dc == 0:
                S.wait("tensor", "ac", sil_done[23])
            elif dc == 1:
                S.wait("tensor", "dv", h_done[23])
            else:
                S.wait("tensor", "dv", x1done[dc - 2])
            for nb in range(2):
                for c in range(16):
                    S.op("tensor", lambda e, c=c, nb=nb, pw=pw, dc=dc: e.matmul(
                        pw[nb][:, :], woutT[:, c, dc * 128:(dc + 1) * 128],
                        ygv[:, c, nb * 512:(nb + 1) * 512],
                        start=(c == 0), stop=(c == 15)), "pe" if (c == 15 and nb == 1) else None)
            S.wait("vector", "pe", S.val("pe"))
            S.wait("vector", "ac", dn_done[dc])
            S.wait("vector", "lx%d" % ib, ld_x[dc])
            if dc == 0:
                S.wait("vector", "s1x", prev_s1x)
            for nb in range(2):
                sl = slice(dc * 1024 + nb * 512, dc * 1024 + (nb + 1) * 512)
                S.op("vector", lambda e, sl=sl, nb=nb, pw=pw: e.tensor_tensor(
                    x1_sb[:, sl], pw[nb][:, :], sg_sb[:, sl], ALU.mult), "dv")
            S.wait("vector", "dv", S.val("dv"))
            x1done[dc] = S.op("vector", lambda e, dc=dc, ib=ib: e.tensor_tensor(
                x1_sb[:, dc * 1024:(dc + 1) * 1024], x1_sb[:, dc * 1024:(dc + 1) * 1024],
                x_sb[ib][:, :], ALU.add), "dv")
            x_used[dc] = x1done[dc]
            S.wait("sync", "dv", x1done[dc])
            S.dma("sync", io["x1"][:, dc, t0:t0 + TB], x1_sb[:, dc * 1024:(dc + 1) * 1024], "s1x")
        prev_s1x = S.val("s1x")
        sq = tmp[:, 0:8192]
        S.wait("scalar", "dv", x1done[7])
        S.wait("scalar", "sx", arena_free)
        sq_done = S.op("scalar", lambda e: e.activation(sq[:, :], x1_sb[:, :], AF.Square), "ac")
        S.wait("tensor", "ac", sq_done)
        for nb in range(2):
            for c in range(8):
                S.op("tensor", lambda e, c=c, nb=nb: e.matmul(
                    psum[nb][0:1, :], ones_f[:, :],
                    x1_sb[:, c * 1024 + nb * 512:c * 1024 + (nb + 1) * 512],
                    start=(c == 0), stop=(c == 7)))
        for nb in range(2):
            for c in range(8):
                S.op("tensor", lambda e, c=c, nb=nb: e.matmul(
                    psum[2 + nb][0:1, :], ones_f[:, :],
                    sq[:, c * 1024 + nb * 512:c * 1024 + (nb + 1) * 512],
                    start=(c == 0), stop=(c == 7)), "pe" if (c == 7 and nb == 1) else None)
        S.wait("vector", "pe", S.val("pe"))
        mean = rows[:, 0:1024]; msq = rows[:, 1024:2048]
        rstd = msq; mrstd = mean
        for nb in range(2):
            S.op("vector", lambda e, nb=nb, mean=mean: e.tensor_scalar(
                mean[:, nb * 512:(nb + 1) * 512], psum[nb][0:1, :], 1.0 / 1024.0, None, ALU.mult), drain=(nb == 1))
            S.op("vector", lambda e, nb=nb, msq=msq: e.tensor_scalar(
                msq[:, nb * 512:(nb + 1) * 512], psum[2 + nb][0:1, :], 1.0 / 1024.0, None, ALU.mult), drain=(nb == 1))
        psg_read = S.op("vector", lambda e: e.tensor_tensor(rowt[:, :], mean[:, :], mean[:, :], ALU.mult), "dv", drain=True)
        S.op("vector", lambda e: e.tensor_sub(rstd[:, :], msq[:, :], rowt[:, :]), "dv", drain=True)

        S.wait("scalar", "dv", S.val("dv"))
        S.op("scalar", lambda e: e.activation(rstd[:, :], rstd[:, :], AF.Ln, bias=epsc[0:1, 0:1]), drain=True)
        rstd_done = S.op("scalar", lambda e: e.activation(rstd[:, :], rstd[:, :], AF.Exp, scale=-0.5), "ac", drain=True)
        S.wait("vector", "ac", rstd_done)
        mr_done = S.op("vector", lambda e: e.tensor_mul(mrstd[:, :], mean[:, :], rstd[:, :]), "dv")
        S.wait("tensor", "dv", mr_done)
        for nb in range(2):
            S.op("tensor", lambda e, nb=nb: e.matmul(
                psum[4 + nb][:, :], ones_r[:, :], rstd[0:1, nb * 512:(nb + 1) * 512],
                start=True, stop=True))
            S.op("tensor", lambda e, nb=nb: e.matmul(
                psum[6 + nb][:, :], ones_r[:, :], mrstd[0:1, nb * 512:(nb + 1) * 512],
                start=True, stop=True), "pe" if nb == 1 else None)
        S.wait("vector", "pe", S.val("pe"))
        for nb in range(2):
            x1v = x1_sb.ap().rearrange("p (c t) -> p c t", t=1024)[:, :, nb * 512:(nb + 1) * 512]
            tv = tmp.rearrange("p (c t) -> p c t", t=1024)[:, 0:8, nb * 512:(nb + 1) * 512]
            S.op("vector", lambda e, x1v=x1v, tv=tv, nb=nb: e.tensor_tensor(
                tv, x1v, psum[4 + nb][:, :].unsqueeze(1).broadcast_to([128, 8, 512]), ALU.mult), drain=True)
            psd_read = S.op("vector", lambda e, tv=tv, nb=nb: e.tensor_tensor(
                tv, tv, psum[6 + nb][:, :].unsqueeze(1).broadcast_to([128, 8, 512]), ALU.subtract), "dv")
        S.wait("sync", "dv", psd_read)
        arena_free = S.dma("sync", io["xn1"][:, :, t0:t0 + TB],
                           tmp[:, 0:8192].rearrange("p (c t) -> p c t", t=1024), "sx")
    S.wait("sync", "sx", S.val("sx"))
    S.wait("sync", "s1x", S.val("s1x"))



def build_phaseA(S, io, alloc, psum):
    T = 256
    winT = alloc("a_winT", [128, 8, 4384], BF16)
    x_sb = alloc("a_x", [128, 8, T], F32)
    xn_sb = alloc("a_xn", [128, 8, T], BF16)
    z_sb = alloc("a_z", [128, 16, T], BF16)
    xbc = alloc("a_xbc", [128, 18, T + 3], F32)
    xssm = alloc("a_xssm", [128, 16, T], BF16)
    b_sb = alloc("a_b", [128, T], BF16)
    c_sb = alloc("a_c", [128, T], BF16)
    dtp = alloc("a_dtp", [32, T], F32)
    dtt = alloc("a_dtt", [32, T], F32)
    la_sb = alloc("a_la", [32, T], F32)
    s_sb = alloc("a_s", [32, 128], F32)
    sT = alloc("a_sT", [128, 32], F32)
    dtpT = alloc("a_dtpT", [128, 32], F32)
    sball = alloc("a_sball", [128, 2048], F32)
    dt_m = alloc("a_dt", [128, 2048], F32)
    e_m = alloc("a_e", [128, 2048], BF16)
    ce_m = alloc("a_ce", [128, 2048], BF16)
    mt_m = alloc("a_mt", [128, 2048], BF16)
    decays = alloc("a_dec", [128, 32], F32)
    x_tm = alloc("a_xtm", [128, 2048], BF16)
    xs1 = alloc("a_xs1", [128, 2048], BF16)
    xs2 = alloc("a_xs2", [128, 1024], BF16)
    btm = alloc("a_btm", [128, 128], BF16)
    gtm = alloc("a_gtm", [128, 128], BF16)
    ht = alloc("a_ht", [128, 2048], F32)
    htb = alloc("a_htb", [128, 2048], BF16)
    y_sb = alloc("a_y", [128, 16, 128], F32)
    tmp = alloc("a_tmp", [128, 2048], F32)
    sz = alloc("a_sz", [128, 16, 128], BF16)
    yg_sb = alloc("a_yg", [128, 16, T], BF16)
    idf = alloc("a_idf", [128, 128], F32)
    idb = alloc("a_idb", [128, 128], BF16)
    tri = alloc("a_tri", [128, 128], BF16)
    convw = alloc("a_convw", [128, 18 * 4], F32)
    convb = alloc("a_convb", [128, 18], F32)
    dcm = alloc("a_dcm", [128, 16], F32)
    nwcm = alloc("a_nwcm", [128, 16], F32)
    dtb = alloc("a_dtb", [32, 1], F32)
    negA = alloc("a_negA", [32, 1], F32)
    zcol = alloc("a_zcol", [32, 1], F32)
    ones_c = alloc("a_onesc", [128, 1], F32)
    ones_r = alloc("a_onesr", [1, 128], F32)
    epsc = alloc("a_eps", [1, 1], F32)
    rows = alloc("a_rows", [1, 4 * T], F32)
    ctmp = [alloc("a_ctmp%d" % i, [128, T], F32) for i in range(2)]
    s_d = S.nc.dram_tensor("a_s_d", [32, 128], F32, kind="Internal").ap()

    mean = rows[:, 0:T]; msq = rows[:, T:2 * T]
    rowt = rows[:, 2 * T:3 * T]; rrow = rows[:, 3 * T:3 * T + 128]

    S.dma("gpsimd", winT[:, :, :], io["winT"].rearrange("(c p) d -> p c d", p=128), "wld")
    S.dma("sync", convw[:, :].rearrange("p (f k) -> p f k", k=4),
          io["convw"].rearrange("f p k -> p f k"), "li")
    S.dma("sync", convb[:, :], io["convb"], "li")
    S.dma("sync", dcm[:, :], io["dcm"], "li")
    S.dma("sync", nwcm[:, :], io["nwcm"], "li")
    S.dma("sync", dtb[:, :], io["dtb"], "li")
    S.dma("sync", negA[:, :], io["alog"], "li")
    S.dma("sync", idf[:, :], io["identf"], "li")
    S.dma("sync", idb[:, :], io["identb"], "li")
    ld_init = S.dma("sync", tri[:, :], io["trimask"], "li")
    S.op("vector", lambda e: e.memset(zcol[:, :], 0.0))
    S.op("vector", lambda e: e.memset(ones_c[:, :], 1.0))
    S.op("vector", lambda e: e.memset(ones_r[:, :], 1.0))
    S.op("vector", lambda e: e.memset(epsc[:, :], EPS), "dv")
    S.wait("scalar", "li", ld_init)
    S.op("scalar", lambda e: e.activation(negA[:, :], negA[:, :], AF.Exp), "ac", drain=True)
    S.wait("vector", "ac", S.val("ac"))
    S.op("vector", lambda e: e.tensor_scalar(negA[:, :], negA[:, :], -1.0, None, ALU.mult), "dv", drain=True)

    S.op("vector", lambda e: e.memset(ht[:, :], 0.0))
    S.op("vector", lambda e: e.memset(htb[:, :], 0.0), "dv")

    evac_done = {}
    prev = {}

    for ch in range(16):
        g0 = ch * T
        seq_start = (ch % 8 == 0)
        S.wait("sync", "dv", prev.get("x_free", 0))
        ld_x = S.dma("sync", x_sb[:, :, :], io["x_fm"][:, :, g0:g0 + T], "lx")
        S.wait("scalar", "lx", ld_x)
        S.wait("scalar", "dv", prev.get("tmp_free", S.val("dv")))
        sq_done = S.op("scalar", lambda e: e.activation(
            tmp[:, :], x_sb.ap().rearrange("p c t -> p (c t)"), AF.Square), "ac")
        S.wait("tensor", "ac", sq_done)
        S.wait("tensor", "dv", prev.get("b67_free", 0))
        for c in range(8):
            S.op("tensor", lambda e, c=c: e.matmul(
                psum[6][0:1, 0:T], ones_c[:, :], x_sb[:, c, :], start=(c == 0), stop=(c == 7)))
        for c in range(8):
            S.op("tensor", lambda e, c=c: e.matmul(
                psum[7][0:1, 0:T], ones_c[:, :], tmp[:, c * T:(c + 1) * T],
                start=(c == 0), stop=(c == 7)), "pe" if c == 7 else None)
        S.wait("vector", "pe", S.val("pe"))
        S.op("vector", lambda e: e.tensor_scalar(mean, psum[6][0:1, 0:T], 1.0 / 1024.0, None, ALU.mult), drain=True)
        S.op("vector", lambda e: e.tensor_scalar(msq, psum[7][0:1, 0:T], 1.0 / 1024.0, None, ALU.mult), drain=True)
        S.op("vector", lambda e: e.tensor_tensor(rowt, mean, mean, ALU.mult), drain=True)
        S.op("vector", lambda e: e.tensor_sub(rowt, msq, rowt), "dv", drain=True)
        S.wait("scalar", "dv", S.val("dv"))
        S.op("scalar", lambda e: e.activation(rowt, rowt, AF.Ln, bias=epsc[0:1, 0:1]), drain=True)
        S.op("scalar", lambda e: e.activation(rowt, rowt, AF.Exp, scale=-0.5), "ac", drain=True)
        S.wait("vector", "ac", S.val("ac"))
        S.op("vector", lambda e: e.tensor_mul(msq, mean, rowt), "dv", drain=True)
        S.wait("tensor", "dv", S.val("dv"))
        S.op("tensor", lambda e: e.matmul(psum[6][:, 0:T], ones_r[:, :], rowt, start=True, stop=True))
        S.op("tensor", lambda e: e.matmul(psum[7][:, 0:T], ones_r[:, :], msq, start=True, stop=True),
             "pe")
        S.wait("vector", "pe", S.val("pe"))
        S.wait("vector", "sxn", prev.get("xn_stored", 0))
        S.op("vector", lambda e: e.tensor_tensor(
            tmp[:, :].rearrange("p (c t) -> p c t", t=T), x_sb[:, :, :],
            psum[6][:, 0:T].unsqueeze(1).broadcast_to([128, 8, T]), ALU.mult), drain=True)
        xn_done = S.op("vector", lambda e: e.tensor_tensor(
            xn_sb[:, :, :], tmp[:, :].rearrange("p (c t) -> p c t", t=T),
            psum[7][:, 0:T].unsqueeze(1).broadcast_to([128, 8, T]), ALU.subtract), "dv")
        prev["b67_free"] = xn_done
        prev["x_free"] = xn_done
        S.wait("sync", "dv", xn_done)
        prev["xn_stored"] = S.dma("sync", io["xn0"][:, :, g0:g0 + T], xn_sb[:, :, :], "sxn")

        S.wait("vector", "dv", S.val("dv"))
        if seq_start:
            S.op("vector", lambda e: e.memset(xbc[:, :, 0:3], 0.0), "dv", drain=True)
        else:
            S.op("vector", lambda e: e.tensor_copy(xbc[:, :, 0:3], xbc[:, :, T:T + 3]), "dv", drain=True)
        halo_done = S.val("dv")

        S.wait("tensor", "wld", S.val("wld"))
        S.wait("tensor", "dv", xn_done)
        dt_raw_done = None
        for dc in range(35):
            bank = dc % 6
            S.wait("tensor", "dv", evac_done.get(("dv", bank), 0))
            S.wait("tensor", "ac", evac_done.get(("ac", bank), 0))
            M = 32 if dc == 34 else 128
            for c in range(8):
                S.op("tensor", lambda e, c=c, dc=dc, bank=bank, M=M: e.matmul(
                    psum[bank][0:M, 0:T], winT[:, c, dc * 128:dc * 128 + M], xn_sb[:, c, :],
                    start=(c == 0), stop=(c == 7)), "pe" if c == 7 else None)
            mmv = S.val("pe")
            if dc < 16:
                S.wait("vector", "pe", mmv)
                evac_done[("dv", bank)] = S.op("vector", lambda e, dc=dc, bank=bank: e.tensor_copy(
                    z_sb[:, dc, :], psum[bank][:, 0:T]), "dv")
            elif dc < 34:
                fc = dc - 16
                S.wait("vector", "pe", mmv)
                if fc == 0:
                    S.wait("vector", "dv", halo_done)
                evac_done[("dv", bank)] = S.op("vector", lambda e, fc=fc, bank=bank: e.tensor_copy(
                    xbc[:, fc, 3:T + 3], psum[bank][:, 0:T]), "dv")
            else:
                S.wait("vector", "pe", mmv)
                S.op("vector", lambda e, bank=bank: e.tensor_scalar(
                    dtp[:, :], psum[bank][0:32, 0:T], dtb[:, 0:1], None, ALU.add), drain=True)
                evac_done[("dv", bank)] = S.op("vector", lambda e: e.tensor_scalar(
                    dtt[:, :], dtp[:, :], 0.0, None, ALU.max), "dv", drain=True)
                dt_raw_done = evac_done[("dv", bank)]
        S.wait("scalar", "dv", dt_raw_done)
        S.op("scalar", lambda e: e.activation(la_sb[:, :], dtp[:, :], AF.Abs), drain=True)
        S.op("scalar", lambda e: e.activation(la_sb[:, :], la_sb[:, :], AF.Exp, scale=-1.0), drain=True)
        sp_done = S.op("scalar", lambda e: e.activation(la_sb[:, :], la_sb[:, :], AF.Ln, bias=1.0), "ac", drain=True)
        S.wait("vector", "ac", sp_done)
        S.op("vector", lambda e: e.tensor_add(dtp[:, :], dtt[:, :], la_sb[:, :]), drain=True)
        la_done = S.op("vector", lambda e: e.tensor_scalar(
            la_sb[:, :], dtp[:, :], negA[:, 0:1], None, ALU.mult), "dv", drain=True)

        conv_sil = {}
        S.wait("vector", "dv", S.val("dv"))
        for fc in range(18):
            cb = ctmp[fc % 2]
            wc = lambda k, fc=fc: convw[:, fc * 4 + k:fc * 4 + k + 1]
            S.wait("vector", "ac", conv_sil.get(fc - 2, 0))
            S.op("vector", lambda e, fc=fc, wc=wc, cb=cb: e.tensor_scalar(
                cb[:, :], xbc[:, fc, 0:T], wc(0), None, ALU.mult), drain=True)
            for k in (1, 2, 3):
                S.op("vector", lambda e, fc=fc, k=k, wc=wc, cb=cb: e.scalar_tensor_tensor(
                    cb[:, :], xbc[:, fc, k:T + k], wc(k), cb[:, :], ALU.mult, ALU.add),
                    "dv" if k == 3 else None, drain=True)
            S.wait("scalar", "dv", S.val("dv"))
            if fc < 16:
                dst = xssm[:, fc, :]
            elif fc == 16:
                dst = b_sb[:, :]
            else:
                dst = c_sb[:, :]
            conv_sil[fc] = S.op("scalar", lambda e, dst=dst, fc=fc, cb=cb: e.activation(
                dst, cb[:, :], AF.Silu, bias=convb[:, fc:fc + 1]), "ac")
        conv_done = S.val("ac")

        for q in range(2):
            sc_pe0 = S.val("pe")
            S.wait("vector", "ac", sp_done)
            scan_done = S.op("vector", lambda e, q=q: e.tensor_tensor_scan(
                s_sb[:, :], la_sb[:, q * 128:(q + 1) * 128],
                zcol[:, 0:1].broadcast_to([32, 128]), 0.0, ALU.add, ALU.add), "dv", drain=True)
            S.wait("sync", "dv", scan_done)
            s_st = S.dma("sync", s_d, s_sb[:, :], "ss")
            S.wait("tensor", "dv", max(scan_done, prev.get("xt_free", 0), prev.get("sdt_free", 0)))
            S.op("tensor", lambda e: e.transpose(psum[0].ap().bitcast(F32)[:, 0:32],
                                                 s_sb[:, :], idf[0:32, 0:32]))
            S.op("tensor", lambda e, q=q: e.transpose(psum[1].ap().bitcast(F32)[:, 0:32],
                                                      dtp[:, q * 128:(q + 1) * 128], idf[0:32, 0:32]), "pe")
            S.wait("vector", "pe", S.val("pe"))
            S.op("vector", lambda e: e.tensor_copy(sT[:, :], psum[0].ap().bitcast(F32)[:, 0:32]), drain=True)
            sdt_done = S.op("vector", lambda e: e.tensor_copy(dtpT[:, :], psum[1].ap().bitcast(F32)[:, 0:32]), "dv", drain=True)
            prev["sdt_free"] = sdt_done
            S.wait("tensor", "ac", conv_done)
            S.wait("tensor", "dv", sdt_done)
            xt_evac = {}
            for fc in range(16):
                bank = fc % 2
                S.wait("tensor", "dv", xt_evac.get(fc - 2, 0))
                S.op("tensor", lambda e, fc=fc, bank=bank, q=q: e.transpose(
                    psum[bank].ap().bitcast(BF16)[:, 0:128],
                    xssm[:, fc, q * 128:(q + 1) * 128], idb[:, :]), "pe")
                S.wait("vector", "pe", S.val("pe"))
                xt_evac[fc] = S.op("vector", lambda e, fc=fc, bank=bank: e.tensor_copy(
                    x_tm[:, fc * 128:(fc + 1) * 128], psum[bank].ap().bitcast(BF16)[:, 0:128]), "dv", drain=True)
            prev["xt_free"] = xt_evac[15]
            S.op("tensor", lambda e, q=q: e.transpose(
                psum[2].ap().bitcast(BF16)[:, 0:128], b_sb[:, q * 128:(q + 1) * 128], idb[:, :]))
            S.op("tensor", lambda e, q=q: e.matmul(
                psum[3][:, 0:128], b_sb[:, q * 128:(q + 1) * 128], c_sb[:, q * 128:(q + 1) * 128],
                start=True, stop=True), "pe")
            gt_mm = S.val("pe")
            S.wait("vector", "pe", gt_mm)
            btm_done = S.op("vector", lambda e: e.tensor_copy(btm[:, :], psum[2].ap().bitcast(BF16)[:, 0:128]), "dv", drain=True)
            gtm_done = S.op("vector", lambda e: e.tensor_mul(gtm[:, :], psum[3][:, 0:128], tri[:, :]), "dv", drain=True)
            S.wait("vector", "dv", xt_evac[15])
            xs1_done = S.op("vector", lambda e: e.tensor_tensor(
                xs1[:, :].rearrange("p (h w) -> p h w", w=64), x_tm[:, :].rearrange("p (h w) -> p h w", w=64),
                dtpT[:, :].unsqueeze(2).broadcast_to([128, 32, 64]), ALU.mult), "dv")

            for hh in range(2):
                S.wait("sync", "ss", s_st)
                S.wait("sync", "ac", prev.get("sball_free_ac", 0))
                S.wait("sync", "dv", prev.get("sball_free_dv", 0))
                srow = s_d.rearrange("h j -> (h j)")[hh * 2048:(hh + 1) * 2048]
                ld_sb = S.dma("sync", sball[:, :].unsqueeze(1),
                              srow.unsqueeze(0).partition_broadcast(128), "lsb")
                S.wait("vector", "lsb", ld_sb)
                S.wait("vector", "ac", prev.get("dt_free_ac", 0))
                S.op("vector", lambda e, hh=hh: e.tensor_tensor(
                    dt_m[:, :].rearrange("p (h i) -> p h i", i=128),
                    sball[:, :].rearrange("p (h i) -> p h i", i=128),
                    sT[:, hh * 16:(hh + 1) * 16].unsqueeze(2).broadcast_to([128, 16, 128]),
                    ALU.subtract), drain=True)
                dt_done = S.op("vector", lambda e: e.tensor_scalar(
                    dt_m[:, :], dt_m[:, :], 0.0, None, ALU.min), "dv")
                S.wait("scalar", "dv", dt_done)
                epos_done = S.op("scalar", lambda e: e.activation(sball[:, :], sball[:, :], AF.Exp), "ac")
                prev["sball_free_ac"] = epos_done
                e_done = S.op("scalar", lambda e: e.activation(e_m[:, :], dt_m[:, :], AF.Exp), "ac")
                prev["dt_free_ac"] = e_done
                S.wait("vector", "ac", e_done)
                ce_done = S.op("vector", lambda e, q=q: e.tensor_tensor(
                    ce_m[:, :].rearrange("p (h i) -> p h i", i=128),
                    c_sb[:, q * 128:(q + 1) * 128].unsqueeze(1).broadcast_to([128, 16, 128]),
                    sball[:, :].rearrange("p (h i) -> p h i", i=128), ALU.mult), "dv")
                dec_done = S.op("vector", lambda e, hh=hh: e.tensor_copy(
                    decays[:, hh * 16:(hh + 1) * 16].unsqueeze(2),
                    sball[:, :].rearrange("p (h i) -> p h i", i=128)[:, :, 127:128]), "dv", drain=True)
                prev["sball_free_dv"] = dec_done
                S.wait("vector", "dv", gtm_done)
                mt_done = S.op("vector", lambda e: e.tensor_tensor(
                    mt_m[:, :].rearrange("p (h i) -> p h i", i=128),
                    e_m[:, :].rearrange("p (h i) -> p h i", i=128),
                    gtm[:, :].unsqueeze(1).broadcast_to([128, 16, 128]), ALU.mult), "dv")
                xs2_done = S.op("vector", lambda e, hh=hh: e.tensor_tensor(
                    xs2[:, :].rearrange("p (h w) -> p h w", w=64),
                    xs1[:, hh * 1024:(hh + 1) * 1024].rearrange("p (h w) -> p h w", w=64),
                    e_m[:, :].rearrange("p (h i) -> p h i", i=128)[:, :, 127:128].broadcast_to([128, 16, 64]),
                    ALU.mult), "dv")
                S.wait("tensor", "dv", mt_done)
                S.wait("tensor", "dv", prev.get("y_free", 0))
                for hp in range(16):
                    h = hh * 16 + hp
                    bank = 4 + hp // 8
                    quad = (hp // 2) % 4
                    pslot = hp % 2
                    S.op("tensor", lambda e, h=h, hp=hp, bank=bank, quad=quad, pslot=pslot: e.matmul(
                        psum[bank][pslot * 64:(pslot + 1) * 64, quad * 128:(quad + 1) * 128],
                        xs1[:, h * 64:(h + 1) * 64], mt_m[:, hp * 128:(hp + 1) * 128],
                        start=True, stop=False))
                    S.op("tensor", lambda e, h=h, hp=hp, bank=bank, quad=quad, pslot=pslot: e.matmul(
                        psum[bank][pslot * 64:(pslot + 1) * 64, quad * 128:(quad + 1) * 128],
                        htb[:, h * 64:(h + 1) * 64], ce_m[:, hp * 128:(hp + 1) * 128],
                        start=False, stop=True), "pe" if hp == 15 else None)
                y_mm = S.val("pe")
                S.wait("tensor", "dv", xs2_done)
                S.wait("tensor", "dv", prev.get("dh_free", 0))
                for hp in range(16):
                    h = hh * 16 + hp
                    bank = 6 + hp // 8
                    S.op("tensor", lambda e, h=h, hp=hp, bank=bank: e.matmul(
                        psum[bank][:, (hp % 8) * 64:(hp % 8 + 1) * 64],
                        btm[:, :], xs2[:, hp * 64:(hp + 1) * 64], start=True, stop=True),
                        "pe" if hp == 15 else None)
                dh_mm = S.val("pe")
                S.wait("vector", "pe", y_mm)
                S.op("vector", lambda e, hh=hh, q=q: e.tensor_tensor(
                    tmp[:, 0:1024].rearrange("p (f i) -> p f i", i=128),
                    xssm[:, hh * 8:(hh + 1) * 8, q * 128:(q + 1) * 128],
                    dcm[:, hh * 8:(hh + 1) * 8].unsqueeze(2).broadcast_to([128, 8, 128]), ALU.mult), drain=True)
                for bk in range(2):
                    S.op("vector", lambda e, bk=bk, hh=hh: e.tensor_tensor(
                        y_sb[:, hh * 8 + bk * 4:hh * 8 + (bk + 1) * 4, :],
                        tmp[:, bk * 512:(bk + 1) * 512].rearrange("p (f i) -> p f i", i=128),
                        psum[4 + bk][:, :].rearrange("p (f i) -> p f i", i=128), ALU.add),
                        "dv" if bk == 1 else None)
                prev["y_free"] = S.val("dv")
                S.wait("vector", "pe", dh_mm)
                S.wait("vector", "dv", dec_done)
                S.op("vector", lambda e, hh=hh: e.tensor_tensor(
                    ht[:, hh * 1024:(hh + 1) * 1024].rearrange("p (h w) -> p h w", w=64),
                    ht[:, hh * 1024:(hh + 1) * 1024].rearrange("p (h w) -> p h w", w=64),
                    decays[:, hh * 16:(hh + 1) * 16].unsqueeze(2).broadcast_to([128, 16, 64]), ALU.mult), drain=True)
                for bk in range(2):
                    S.op("vector", lambda e, bk=bk, hh=hh: e.tensor_tensor(
                        ht[:, hh * 1024 + bk * 512:hh * 1024 + (bk + 1) * 512],
                        ht[:, hh * 1024 + bk * 512:hh * 1024 + (bk + 1) * 512],
                        psum[6 + bk][:, :], ALU.add), drain=True)
                prev["dh_free"] = S.op("vector", lambda e, hh=hh: e.tensor_copy(
                    htb[:, hh * 1024:(hh + 1) * 1024], ht[:, hh * 1024:(hh + 1) * 1024]), "dv")
            S.wait("scalar", "dv", prev["y_free"])
            ysq_done = S.op("scalar", lambda e: e.activation(
                tmp[:, :], y_sb[:, :, :].rearrange("p f i -> p (f i)"), AF.Square), "ac")
            S.wait("tensor", "ac", ysq_done)
            S.wait("tensor", "dv", prev["dh_free"])
            for fc in range(16):
                S.op("tensor", lambda e, fc=fc: e.matmul(
                    psum[6][0:1, 0:128], ones_c[:, :], tmp[:, fc * 128:(fc + 1) * 128],
                    start=(fc == 0), stop=(fc == 15)), "pe" if fc == 15 else None)
            S.wait("scalar", "pe", S.val("pe"))
            S.op("scalar", lambda e: e.activation(rrow, psum[6][0:1, 0:128], AF.Ln,
                                                  scale=1.0 / 2048.0, bias=epsc[0:1, 0:1]), drain=True)
            rms_done = S.op("scalar", lambda e: e.activation(rrow, rrow, AF.Exp, scale=-0.5), "ac", drain=True)
            S.wait("tensor", "ac", rms_done)
            S.op("tensor", lambda e: e.matmul(psum[7][:, 0:128], ones_r[:, :], rrow,
                                              start=True, stop=True), "pe")
            rstd_b = S.val("pe")
            S.wait("scalar", "dv", S.val("dv"))
            sz_done = S.op("scalar", lambda e, q=q: e.activation(
                sz[:, :, :], z_sb[:, :, q * 128:(q + 1) * 128], AF.Silu), "ac")
            S.wait("vector", "pe", rstd_b)
            S.op("vector", lambda e: e.tensor_tensor(
                dt_m[:, :].rearrange("p (f i) -> p f i", i=128), y_sb[:, :, :],
                nwcm[:, :].unsqueeze(2).broadcast_to([128, 16, 128]), ALU.mult), drain=True)
            S.op("vector", lambda e: e.tensor_tensor(
                dt_m[:, :].rearrange("p (f i) -> p f i", i=128),
                dt_m[:, :].rearrange("p (f i) -> p f i", i=128),
                psum[7][:, 0:128].unsqueeze(1).broadcast_to([128, 16, 128]), ALU.mult), "dv", drain=True)
            S.wait("vector", "ac", sz_done)
            S.wait("vector", "syg", prev.get("yg_stored", 0))
            yg_done = S.op("vector", lambda e, q=q: e.tensor_tensor(
                yg_sb[:, :, q * 128:(q + 1) * 128],
                dt_m[:, :].rearrange("p (f i) -> p f i", i=128), sz[:, :, :], ALU.mult), "dv")
            prev["b67_free"] = S.val("dv")
            prev["tmp_free"] = S.val("dv")
        S.wait("sync", "dv", S.val("dv"))
        prev["yg_stored"] = S.dma("sync", io["yg"][:, :, g0:g0 + T], yg_sb[:, :, :], "syg")
        for b in range(6):
            evac_done[("dv", b)] = S.val("dv")
        if ch == 7:
            S.wait("vector", "pe", S.val("pe"))
            S.op("vector", lambda e: e.memset(ht[:, :], 0.0))
            S.op("vector", lambda e: e.memset(htb[:, :], 0.0), "dv")
    S.wait("sync", "sxn", S.val("sxn"))
    S.wait("sync", "syg", S.val("syg"))
    S.wait("sync", "ss", S.val("ss"))



def build_phaseC(S, io, alloc, psum):
    q_fm = alloc("c_q", [128, 8, 2048], BF16)
    k_fm = alloc("c_k", [128, 8, 2048], BF16)
    vaug = alloc("c_v", [128, 16, 8 * 65 * 2], BF16)
    aout = alloc("c_ao", [128, 8, 2048], BF16)
    xn_sb = alloc("c_xn", [128, 8, 2048], BF16)
    w_sb = alloc("c_w", [128, 8, 1024], BF16)
    stexp = [alloc("c_se%d" % i, [128, 512], BF16) for i in range(4)]
    abias = alloc("c_ab", [128, 256], F32)
    cmask = alloc("c_cm", [128, 4, 512], BF16)
    rbrow = alloc("c_rb", [1, 512], F32)
    rbb = alloc("c_rbb", [64, 512], F32)
    ones_r = alloc("c_ones", [1, 64], F32)
    ev = alloc("c_ev", [128, 512], F32)

    S.op("vector", lambda e: e.memset(ones_r[:, :], 1.0), "dv")
    S.dma("sync", abias[:, :], io["abias"], "ld")
    ld_cm = S.dma("sync", cmask[:, :, :], io["cmask"].rearrange("d p t -> p d t"), "ld")
    S.wait("scalar", "ld", ld_cm)

    prev = {}
    for b in range(2):
        c0 = b * 2048
        S.wait("gpsimd", "pe", prev.get("xn_free", 0))
        ld_xn = S.dma("gpsimd", xn_sb[:, :, :], io["xn1"][:, :, c0:c0 + 2048], "cxn")
        S.wait("vector", "pe", prev.get("v_used", 0))
        vset = S.op("vector", lambda e: e.memset(vaug[:, :, :], 1.0), "dv")
        for wi, wname in enumerate(("wqT", "wkT", "wvT")):
            S.wait("gpsimd", "pe", prev.get("w_used", 0))
            ldw = S.dma("gpsimd", w_sb[:, :, :],
                        io[wname].rearrange("(c p) d -> p c d", p=128), "cw")
            S.wait("tensor", "cw", ldw)
            if wi == 0:
                S.wait("tensor", "cxn", ld_xn)
            dst = q_fm if wi == 0 else k_fm
            if wi < 2:
                for dc in range(8):
                    for tb4 in range(4):
                        bank = 6 + (dc * 4 + tb4) % 2
                        S.wait("tensor", "dv", prev.get(("qk", bank), 0))
                        for c in range(8):
                            S.op("tensor", lambda e, dc=dc, tb4=tb4, c=c, bank=bank: e.matmul(
                                psum[bank][:, :], w_sb[:, c, dc * 128:(dc + 1) * 128],
                                xn_sb[:, c, tb4 * 512:(tb4 + 1) * 512],
                                start=(c == 0), stop=(c == 7)), "pe" if c == 7 else None)
                        S.wait("vector", "pe", S.val("pe"))
                        prev[("qk", bank)] = S.op("vector", lambda e, dst=dst, dc=dc, tb4=tb4, bank=bank:
                            e.tensor_copy(dst[:, dc, tb4 * 512:(tb4 + 1) * 512], psum[bank][:, :]), "dv")
            else:
                S.wait("tensor", "dv", vset)
                for kt in range(16):
                    for db in range(2):
                        bank = 6 + (kt * 2 + db) % 2
                        S.wait("tensor", "dv", prev.get(("qk", bank), 0))
                        for c in range(8):
                            S.op("tensor", lambda e, kt=kt, db=db, c=c, bank=bank: e.matmul(
                                psum[bank][:, :], xn_sb[:, c, kt * 128:(kt + 1) * 128],
                                w_sb[:, c, db * 512:(db + 1) * 512],
                                start=(c == 0), stop=(c == 7)), "pe" if c == 7 else None)
                        S.wait("vector", "pe", S.val("pe"))
                        prev[("qk", bank)] = S.op("vector", lambda e, kt=kt, db=db, bank=bank:
                            e.tensor_copy(
                                vaug[:, kt, db * 520:db * 520 + 520].rearrange(
                                    "p (h w) -> p h w", w=65)[:, :, 0:64],
                                psum[bank][:, :].rearrange("p (h w) -> p h w", w=64)), "dv")
            w_used_v = S.val("pe")
            prev["w_used"] = w_used_v
        prev["xn_free"] = S.val("pe")
        qkv_done = S.val("dv")
        se_act = {}
        se_used = {}
        st_used = {}
        for h in range(16):
            hb = (h % 2) * 64
            hc = h // 2
            for qb in range(4):
                kts = 4 * qb + 4
                pvb = 2 + qb % 2
                S.wait("tensor", "dv", prev.get(("pv", pvb), 0))
                for kt in range(kts):
                    sb_ = kt % 4
                    stb = (0, 1, 6, 7)[sb_]
                    S.wait("tensor", "ac", st_used.get(sb_, 0))
                    if sb_ >= 2:
                        S.wait("tensor", "dv", prev.get(("qk", stb), 0))
                        prev[("qk", stb)] = 0
                    if h == 0 and qb == 0 and kt == 0:
                        S.wait("tensor", "dv", qkv_done)
                    S.op("tensor", lambda e, h=h, qb=qb, kt=kt, stb=stb, hb=hb, hc=hc: e.matmul(
                        psum[stb][:, :], k_fm[hb:hb + 64, hc, kt * 128:(kt + 1) * 128],
                        q_fm[hb:hb + 64, hc, qb * 512:(qb + 1) * 512],
                        start=True, stop=True), "pe")
                    st_mm = S.val("pe")
                    S.wait("scalar", "pe", st_mm)
                    S.wait("scalar", "pe", se_used.get(sb_, 0))
                    st_used[sb_] = S.op("scalar", lambda e, h=h, kt=kt, sb_=sb_, stb=stb: e.activation(
                        stexp[sb_][:, :], psum[stb][:, :], AF.Exp, scale=0.125,
                        bias=abias[:, h * 16 + kt:h * 16 + kt + 1]), "ac")
                    se_act[sb_] = st_used[sb_]
                    dg = kt - 4 * qb
                    if dg >= 0:
                        S.wait("vector", "ac", se_act[sb_])
                        mk_done = S.op("vector", lambda e, dg=dg, sb_=sb_: e.tensor_mul(
                            stexp[sb_][:, :], stexp[sb_][:, :], cmask[:, dg, :]), "dv")
                        S.wait("tensor", "dv", mk_done)
                    else:
                        S.wait("tensor", "ac", st_used[sb_])
                    S.op("tensor", lambda e, h=h, kt=kt, sb_=sb_, kts=kts, pvb=pvb: e.matmul(
                        psum[pvb][0:65, :], vaug[:, kt, h * 65:h * 65 + 65], stexp[sb_][:, :],
                        start=(kt == 0), stop=(kt == kts - 1)), "pe")
                    se_used[sb_] = S.val("pe")
                pv_mm = S.val("pe")
                S.wait("vector", "pe", pv_mm)
                rc = S.op("vector", lambda e, pvb=pvb: e.reciprocal(rbrow[:, :], psum[pvb][64:65, :]), "dv", drain=True)
                S.wait("tensor", "dv", rc)
                bcb = 4 + qb % 2
                bc = S.op("tensor", lambda e, bcb=bcb: e.matmul(
                    psum[bcb][0:64, :], ones_r[:, :], rbrow[0:1, :],
                    start=True, stop=True), "pe")
                S.wait("vector", "pe", bc)
                S.op("vector", lambda e, bcb=bcb: e.tensor_copy(rbb[:, :], psum[bcb][0:64, :]), drain=True)
                prev[("pv", pvb)] = S.op("vector", lambda e, h=h, qb=qb, pvb=pvb, hb=hb, hc=hc:
                    e.tensor_mul(aout[hb:hb + 64, hc, qb * 512:(qb + 1) * 512],
                                 psum[pvb][0:64, :], rbb[:, :]), "dv")
        prev["v_used"] = S.val("pe")
        S.wait("gpsimd", "pe", prev.get("w_used", 0))
        ldw = S.dma("gpsimd", w_sb[:, :, :], io["woT"].rearrange("(c p) d -> p c d", p=128), "cw")
        S.wait("tensor", "cw", ldw)
        S.wait("tensor", "dv", S.val("dv"))
        S.wait("tensor", "ac", st_used.get(2, 0))
        S.wait("tensor", "ac", st_used.get(3, 0))
        for dc in range(8):
            for tb4 in range(4):
                bank = 6 + (dc * 4 + tb4) % 2
                S.wait("tensor", "dv", prev.get(("s1ev", bank), 0))
                for c in range(8):
                    S.op("tensor", lambda e, dc=dc, tb4=tb4, c=c, bank=bank: e.matmul(
                        psum[bank][:, :], w_sb[:, c, dc * 128:(dc + 1) * 128],
                        aout[:, c, tb4 * 512:(tb4 + 1) * 512],
                        start=(c == 0), stop=(c == 7)), "pe" if c == 7 else None)
                S.wait("vector", "pe", S.val("pe"))
                S.wait("vector", "st", prev.get("ev_st", 0))
                evd = S.op("vector", lambda e, bank=bank: e.tensor_copy(ev[:, :], psum[bank][:, :]), "dv")
                S.wait("sync", "dv", evd)
                prev["ev_st"] = S.dma("sync", io["s1"][:, dc, c0 + tb4 * 512:c0 + (tb4 + 1) * 512],
                                      ev[:, :], "st")
                prev[("s1ev", bank)] = evd
        prev["w_used"] = S.val("pe")
        prev[("qk", 6)] = prev[("s1ev", 6)]
        prev[("qk", 7)] = prev[("s1ev", 7)]
    S.wait("sync", "st", S.val("st"))



def build_phaseD(S, io, alloc, psum):
    TB = 1024
    xn_sb = alloc("d_xn", [128, 8, TB], BF16)
    h_sb = alloc("d_h", [128, 32 * TB], BF16)
    acc = alloc("d_acc", [128, 8, TB], F32)
    wb = alloc("d_wb", [128, TB], F32)
    w1s = [alloc("d_w1%d" % i, [128, 8, 256], BF16) for i in range(2)]
    w2s = [alloc("d_w2%d" % i, [128, 32, 128], BF16) for i in range(2)]
    sil = [alloc("d_sil%d" % i, [128, 512], BF16) for i in range(4)]
    tmp2 = alloc("d_tmp2", [128, TB], F32)
    ev = alloc("d_ev", [128, TB], F32)
    b1g = alloc("d_b1g", [128, 8, 32], F32)
    b1u = alloc("d_b1u", [128, 8, 32], F32)
    b2a = alloc("d_b2", [128, 8, 8], F32)
    idf = alloc("d_idf", [128, 128], F32)
    xr = [alloc("d_xr%d" % i, [128, 8, 128], F32) for i in range(2)]
    rw_sb = alloc("d_rw", [128, 8, 8], F32)
    rb_sb = alloc("d_rb", [1, 8], F32)
    onr = alloc("d_onr", [1, 128], F32)
    lt = alloc("d_lt", [128, 8], F32)
    ltw = alloc("d_ltw", [128, 8], F32)
    mrow = alloc("d_m", [128, 4], F32)
    wn_fm = alloc("d_wnfm", [8, TOK], F32)
    wn_d = S.nc.dram_tensor("d_wn_d", [8, TOK], F32, kind="Internal").ap()

    onr_done = S.op("vector", lambda e: e.memset(onr[:, :], 1.0), "dv")
    S.dma("sync", b1g[:, :, :], io["b1g"].rearrange("e p k -> p e k"), "din")
    S.dma("sync", b1u[:, :, :], io["b1u"].rearrange("e p k -> p e k"), "din")
    S.dma("sync", b2a[:, :, :], io["b2r"].rearrange("e p k -> p e k"), "din")
    S.dma("sync", idf[:, :], io["identf"], "din")
    S.dma("sync", rw_sb[:, :, :], io["rwT"].rearrange("(c p) e -> p c e", p=128), "din")
    ld_init = S.dma("sync", rb_sb[:, :], io["rb"], "din")

    xr_used = [0, 0]
    wn_done = {}
    for tt_ in range(32):
        ib = tt_ % 2
        S.wait("sync", "pe", xr_used[ib])
        ldx = S.dma("sync", xr[ib][:, :, :], io["xn1"][:, :, tt_ * 128:(tt_ + 1) * 128], "xr%d" % ib)
        bank = tt_ % 2
        S.wait("tensor", "xr%d" % ib, ldx)
        if tt_ < 2:
            S.wait("tensor", "din", ld_init)
            S.wait("tensor", "dv", onr_done)
        S.wait("tensor", "dv", wn_done.get(tt_ - 2, 0))
        for c in range(8):
            S.op("tensor", lambda e, c=c, ib=ib, bank=bank: e.matmul(
                psum[bank][0:128, 0:8], xr[ib][:, c, :], rw_sb[:, c, :],
                start=(c == 0), stop=False))
        S.op("tensor", lambda e, ib=ib, bank=bank: e.matmul(
            psum[bank][0:128, 0:8], onr[:, :], rb_sb[:, :], start=False, stop=True), "pe")
        xr_used[ib] = S.val("pe")
        lg_mm = S.val("pe")
        S.wait("vector", "pe", lg_mm)
        S.op("vector", lambda e, bank=bank: e.tensor_copy(lt[:, :], psum[bank][0:128, 0:8]), drain=True)
        S.op("vector", lambda e: e.reduce_max(mrow[:, 0:1], lt[:, :], AX.X), drain=True)
        S.op("vector", lambda e: e.tensor_scalar(ltw[:, :], lt[:, :], mrow[:, 0:1], None, ALU.is_equal),
             drain=True)
        S.op("vector", lambda e: e.scalar_tensor_tensor(ltw[:, :], ltw[:, :], -1e9, lt[:, :],
                                                        ALU.mult, ALU.add), drain=True)
        S.op("vector", lambda e: e.reduce_max(mrow[:, 1:2], ltw[:, :], AX.X), drain=True)
        S.op("vector", lambda e: e.tensor_scalar(ltw[:, :], lt[:, :], mrow[:, 1:2], None, ALU.is_ge),
             drain=True)
        S.op("vector", lambda e: e.tensor_scalar(mrow[:, 2:3], mrow[:, 0:1], -1.0, None, ALU.mult),
             "dv", drain=True)
        S.wait("scalar", "dv", S.val("dv"))
        S.op("scalar", lambda e: e.activation(lt[:, :], lt[:, :], AF.Exp, bias=mrow[:, 2:3]),
             "ac", drain=True)
        S.wait("vector", "ac", S.val("ac"))
        S.op("vector", lambda e: e.tensor_mul(lt[:, :], lt[:, :], ltw[:, :]), drain=True)
        S.op("vector", lambda e: e.reduce_sum(mrow[:, 3:4], lt[:, :], AX.X), drain=True)
        S.op("vector", lambda e: e.reciprocal(mrow[:, 2:3], mrow[:, 3:4]), drain=True)
        tw_done = S.op("vector", lambda e: e.tensor_scalar(
            lt[:, :], lt[:, :], mrow[:, 2:3], None, ALU.mult), "dv", drain=True)
        S.wait("tensor", "dv", tw_done)
        S.op("tensor", lambda e, bank=bank: e.transpose(
            psum[2 + bank][0:8, 0:128], lt[:, :], idf[:, :]), "pe")
        tr_mm = S.val("pe")
        S.wait("vector", "pe", tr_mm)
        wn_done[tt_] = S.op("vector", lambda e, tt_=tt_, bank=bank: e.tensor_copy(
            wn_fm[:, tt_ * 128:(tt_ + 1) * 128], psum[2 + bank][0:8, 0:128]), "dv", drain=True)
    S.wait("sync", "dv", wn_done[31])
    wn_st = S.dma("sync", wn_d, wn_fm[:, :], "swn")

    prev = {}
    for tb in range(4):
        t0 = tb * TB
        S.wait("gpsimd", "pe", prev.get("xn_used", 0))
        ld_xn = S.dma("gpsimd", xn_sb[:, :, :], io["xn1"][:, :, t0:t0 + TB], "dxn")
        for e_ in range(8):
            S.wait("sync", "swn", wn_st)
            S.wait("sync", "dv", prev.get("wb_used", 0))
            ld_wb = S.dma("sync", wb[:, :].unsqueeze(1),
                          wn_d[e_, t0:t0 + TB].unsqueeze(0).partition_broadcast(128), "lwb")
            sil_done = {}
            h_done = {}
            for kc in range(32):
                ib = kc % 2
                S.wait("gpsimd", "pe", prev.get(("w1", ib), 0))
                l1 = S.dma("gpsimd", w1s[ib][:, :, 0:128],
                           io["w1T"][e_].rearrange("(c p) h -> p c h", p=128)[:, :, kc * 128:(kc + 1) * 128],
                           "lw1%d" % ib)
                l2 = S.dma("gpsimd", w1s[ib][:, :, 128:256],
                           io["w1T"][e_].rearrange("(c p) h -> p c h", p=128)[:, :, 4096 + kc * 128:4096 + (kc + 1) * 128],
                           "lw1%d" % ib)
                S.wait("tensor", "lw1%d" % ib, l2)
                if kc == 0 and e_ == 0:
                    S.wait("tensor", "dxn", ld_xn)
                gb = [psum[0], psum[1]] if ib == 0 else [psum[4], psum[5]]
                ub = [psum[2], psum[3]] if ib == 0 else [psum[6], psum[7]]
                if kc >= 2:
                    S.wait("tensor", "ac", sil_done[kc - 2])
                    S.wait("tensor", "dv", h_done[kc - 2])
                else:
                    S.wait("tensor", "ac", prev.get("dn_read", 0))
                    S.wait("tensor", "dv", prev.get("dn_read_dv", max(wn_done[31], 0)))
                for nb in range(2):
                    for c in range(8):
                        S.op("tensor", lambda e, c=c, nb=nb, ib=ib, gb=gb: e.matmul(
                            gb[nb][:, :], w1s[ib][:, c, 0:128],
                            xn_sb[:, c, nb * 512:(nb + 1) * 512],
                            start=(c == 0), stop=(c == 7)))
                for nb in range(2):
                    for c in range(8):
                        S.op("tensor", lambda e, c=c, nb=nb, ib=ib, ub=ub: e.matmul(
                            ub[nb][:, :], w1s[ib][:, c, 128:256],
                            xn_sb[:, c, nb * 512:(nb + 1) * 512],
                            start=(c == 0), stop=(c == 7)), "pe" if (c == 7 and nb == 1) else None)
                prev[("w1", ib)] = S.val("pe")
                S.wait("scalar", "pe", S.val("pe"))
                for nb in range(2):
                    sil_done[kc] = S.op("scalar", lambda e, nb=nb, ib=ib, e_=e_, kc=kc, gb=gb: e.activation(
                        sil[2 * ib + nb][:, :], gb[nb][:, :], AF.Silu,
                        bias=b1g[:, e_, kc:kc + 1]), "ac")
                S.wait("vector", "ac", sil_done[kc])
                for nb in range(2):
                    h_done[kc] = S.op("vector", lambda e, nb=nb, ib=ib, e_=e_, kc=kc, ub=ub: e.scalar_tensor_tensor(
                        h_sb[:, kc * TB + nb * 512:kc * TB + (nb + 1) * 512],
                        ub[nb][:, :], b1u[:, e_, kc:kc + 1],
                        sil[2 * ib + nb][:, :], ALU.add, ALU.mult), "dv")
            S.wait("vector", "lwb", ld_wb)
            hview = h_sb.ap().rearrange("p (c t) -> p c t", t=TB)
            dn_ac = {}
            dn_dv = {}
            for dc in range(8):
                ib = dc % 2
                pd = [psum[0], psum[1]] if ib == 0 else [psum[2], psum[3]]
                S.wait("gpsimd", "pe", prev.get(("w2", ib), 0))
                lw2 = S.dma("gpsimd", w2s[ib][:, :, :],
                            io["w2T"][e_].rearrange("(c p) d -> p c d", p=128)[:, :, dc * 128:(dc + 1) * 128],
                            "lw2%d" % ib)
                S.wait("tensor", "lw2%d" % ib, lw2)
                if dc == 0:
                    S.wait("tensor", "dv", h_done[31])
                    S.wait("tensor", "ac", S.val("ac"))
                elif dc == 1:
                    S.wait("tensor", "dv", h_done[30])
                else:
                    S.wait("tensor", "dv", dn_dv[dc - 2])
                for nb in range(2):
                    for c in range(32):
                        S.op("tensor", lambda e, c=c, nb=nb, ib=ib, pd=pd: e.matmul(
                            pd[nb][:, :], w2s[ib][:, c, :], hview[:, c, nb * 512:(nb + 1) * 512],
                            start=(c == 0), stop=(c == 31)), "pe" if (c == 31 and nb == 1) else None)
                prev[("w2", ib)] = S.val("pe")
                S.wait("vector", "pe", S.val("pe"))
                S.wait("vector", "dv", prev.get("tmp2_rd", 0))
                for nb in range(2):
                    sl = slice(nb * 512, (nb + 1) * 512)
                    S.op("vector", lambda e, sl=sl, nb=nb, pd=pd, e_=e_, dc=dc: e.scalar_tensor_tensor(
                        tmp2[:, sl], pd[nb][:, :], b2a[:, e_, dc:dc + 1], wb[:, sl],
                        ALU.add, ALU.mult), "dv", drain=(nb == 1))
                if e_ == 0:
                    dn_dv[dc] = S.op("vector", lambda e, dc=dc: e.tensor_copy(
                        acc[:, dc, :], tmp2[:, :]), "dv")
                else:
                    S.wait("vector", "dv", prev.get(("acc", dc), 0))
                    dn_dv[dc] = S.op("vector", lambda e, dc=dc: e.tensor_add(
                        acc[:, dc, :], acc[:, dc, :], tmp2[:, :]), "dv")
                prev[("acc", dc)] = dn_dv[dc]
                prev["tmp2_rd"] = dn_dv[dc]
                prev[("dn_last", ib)] = dn_dv[dc]
            prev["dn_read"] = S.val("ac")
            prev["dn_read_dv"] = S.val("dv")
            prev["wb_used"] = S.val("dv")
        prev["xn_used"] = S.val("pe")
        hb32 = h_sb.ap().bitcast(F32)
        x1v = hb32[:, 0:8 * TB].rearrange("p (c t) -> p c t", t=TB)
        s1v = hb32[:, 8 * TB:16 * TB].rearrange("p (c t) -> p c t", t=TB)
        S.wait("sync", "pe", S.val("pe"))
        l_x1 = S.dma("sync", x1v, io["x1"][:, :, t0:t0 + TB], "lxe")
        l_s1 = S.dma("sync", s1v, io["s1"][:, :, t0:t0 + TB], "lse")
        for dc in range(8):
            S.wait("scalar", "dv", S.val("dv"))
            sg_d = S.op("scalar", lambda e, dc=dc: e.activation(
                tmp2[:, :], acc[:, dc, :], AF.Silu), "ac")
            S.wait("vector", "ac", sg_d)
            if dc == 0:
                S.wait("vector", "lse", l_s1)
                S.wait("vector", "lxe", l_x1)
            S.wait("vector", "so", prev.get("o_st", 0))
            S.op("vector", lambda e, dc=dc: e.tensor_mul(tmp2[:, :], s1v[:, dc, :], tmp2[:, :]), drain=True)
            od = S.op("vector", lambda e, dc=dc: e.tensor_add(ev[:, :], x1v[:, dc, :], tmp2[:, :]), "dv")
            S.wait("sync", "dv", od)
            prev["o_st"] = S.dma("sync", io["out_fm"][:, dc, t0:t0 + TB], ev[:, :], "so")
    S.wait("sync", "so", S.val("so"))


BUILDERS = {"P": build_phaseP, "A": build_phaseA, "B": build_phaseB, "C": build_phaseC, "D": build_phaseD}


def run_phase(phase, state_list, n_cores=None):
    nc = bass.Bass("TRN2", target_bir_lowering=False)
    ins, outs = PHASE_IO[phase]
    kinds = {n: "ExternalInput" for n in ins}
    kinds.update({n: "ExternalOutput" for n in outs})
    io = {}
    for n in ins + outs:
        shape, dt = TENSORS[n]
        io[n] = nc.dram_tensor(n, shape, dt, kind=kinds[n]).ap()
    S = Sched(nc)

    def alloc(name, shape, dt):
        return nc.alloc_sbuf_tensor(name, shape, dt)

    psum = [nc.alloc_psum_tensor("ps%d" % i, [128, 512], F32) for i in range(8)]
    BUILDERS[phase](S, io, alloc, psum)
    S.emit()
    n = len(state_list) if n_cores is None else n_cores
    in_maps = [{k: state_list[c][k] for k in ins} for c in range(n)]
    res = run_bass_kernel_spmd(nc, in_maps, core_ids=list(range(n)))
    for c in range(n):
        for k in outs:
            state_list[c][k] = res.results[c][k]
    return res


ARENA_BASE = 16512
ARENA_TOP = 229344
FUSED_INTERNAL = ["xn0", "yg", "x1", "xn1", "s1"]
FUSED_OUT = ["out_fm"]
FUSED_IN = ["x_fm", "winT", "convw", "convb", "dtb", "alog", "dcm", "nwcm", "woutT",
            "wgT", "wuT", "wdT", "wqT", "wkT", "wvT", "woT", "rwT", "rb",
            "w1T", "b1g", "b1u", "w2T", "b2r", "identf", "identb", "trimask",
            "cmask", "abias"]


def build_fused():
    nc = bass.Bass("TRN2", target_bir_lowering=False)
    io = {}
    for n in FUSED_IN:
        shape, dt = TENSORS[n]
        io[n] = nc.dram_tensor(n, shape, dt, kind="ExternalInput").ap()
    for n in FUSED_OUT:
        shape, dt = TENSORS[n]
        io[n] = nc.dram_tensor(n, shape, dt, kind="ExternalOutput").ap()
    for n in FUSED_INTERNAL:
        shape, dt = TENSORS[n]
        io[n] = nc.dram_tensor(n, shape, dt, kind="Internal").ap()
    S = Sched(nc)
    psum = [nc.alloc_psum_tensor("ps%d" % i, [128, 512], F32) for i in range(8)]
    cur = [ARENA_BASE]

    def alloc(name, shape, dt):
        per = 1
        for d in shape[1:]:
            per *= d
        per *= mybir.dt.size(dt)
        off = cur[0]
        cur[0] = (off + per + 31) // 32 * 32
        assert cur[0] <= ARENA_TOP, (name, cur[0])
        return nc.alloc_sbuf_tensor_at(name, shape, dt, offset=off)

    nbar = 0
    for ph in ["A", "B", "C", "D"]:
        cur[0] = ARENA_BASE
        BUILDERS[ph](S, io, alloc, psum)
        if ph != "D":
            nbar += 1
            for e in ("tensor", "vector", "scalar"):
                S.op(e, lambda en: en.drain())
            for e in ENGINES:
                S.op(e, lambda en: en.nop(), "bar")
            for e in ENGINES:
                S.wait(e, "bar", 5 * nbar)
    S.emit()
    return nc


def kernel(**inputs):
    shards, w = host_prep(inputs)
    state = []
    for c in range(NCORES):
        st = dict(w)
        st["x_fm"] = shards[c]
        state.append(st)
    nc = build_fused()
    in_maps = [{k: state[c][k] for k in FUSED_IN} for c in range(NCORES)]
    res = run_bass_kernel_spmd(nc, in_maps, core_ids=list(range(NCORES)))
    outs = []
    for c in range(NCORES):
        o = np.asarray(res.results[c]["out_fm"]).transpose(1, 0, 2).reshape(1024, TOK)
        outs.append(np.stack([o[:, :SEQ].T, o[:, SEQ:].T]))
    return np.concatenate(outs, axis=0).astype(np.float32)
